# version 24
# baseline (speedup 1.0000x reference)
import numpy as np
import ml_dtypes
import concourse.bass as bass
import concourse.mybir as mybir
from concourse.bass_utils import run_bass_kernel_spmd

F32 = mybir.dt.float32
BF16 = mybir.dt.bfloat16
I8 = mybir.dt.int8
U8 = mybir.dt.uint8
AF = mybir.ActivationFunctionType
ALU = mybir.AluOpType
AX = mybir.AxisListType
NPBF = ml_dtypes.bfloat16

D = 2048
QP = 1024
KV = 1365
KVC = 11
QD = 2048
NH = 16
DH = 128
INTER = 8192
NEG = -1e30
EPS = 1e-5
SCALE = 1.0 / float(np.sqrt(np.float64(DH)))
ENGINES = ("tensor", "scalar", "vector", "sync", "gpsimd")


def kvrows(c):
    return 128 if c < KVC - 1 else KV - 128 * (KVC - 1)


class Sem:
    __slots__ = ("h", "n", "name", "kind")

    def __init__(self, h, name):
        self.h, self.n, self.name = h, 0, name
        self.kind = None


class Gate:

    def __init__(self, ctx, name, kind="o"):
        self.s = ctx.sem(name, kind)
        self.free = []

    def pi(self, amt=1):
        return ((self.s, amt),)

    def pw(self):
        w = tuple(self.free)
        self.free = []
        return w

    def cw(self):
        return ((self.s, self.s.n),)

    def freed(self, other):
        self.free.append((other.s, other.s.n))


class Ctx:
    def __init__(self, nc, stack):
        self.nc, self.stack = nc, stack
        self.streams = {e: [] for e in ENGINES}
        self._si = 0
        self.dma_open = {"sync": {}, "gpsimd": {}}
        self.n_inst = 0
        self.sem_pool = []
        self.scopes = []
        self.eng_incs = {e: {} for e in ENGINES}

    def sem(self, name=None, kind="o"):
        s = None
        for idx, cand in enumerate(self.sem_pool):
            if cand.kind is None or cand.kind == kind:
                s = self.sem_pool.pop(idx)
                s.name = name or s.name
                break
        if s is None:
            self._si += 1
            h = self.stack.enter_context(self.nc.semaphore(f"s{self._si}"))
            s = Sem(h, name or f"s{self._si}")
        s.kind = s.kind or kind
        if self.scopes:
            self.scopes[-1].append(s)
        return s

    def push_scope(self):
        self.scopes.append([])

    def pop_scope(self):
        self.sem_pool.extend(self.scopes.pop())

    def emit(self, eng, fn, waits=(), incs=(), track=True):
        assert len(incs) <= 1
        for s, amt in incs:
            assert s.kind == ("g" if (eng == "gpsimd" and amt == 16) else s.kind)
            s.n += amt
            if track:
                self.eng_incs[eng][id(s)] = (s, s.n)
        self.streams[eng].append((fn, tuple(waits), tuple(incs)))
        self.n_inst += 1

    def dma(self, eng, out, in_, waits=(), incs=(), first=True, slow=False):
        assert len(incs) == 1 and incs[0][1] == 16, "dma: exactly one +16 sem"
        sem = incs[0][0]
        want = "g" if eng == "gpsimd" else "o"
        assert sem.kind == want, f"sem {sem.name} kind {sem.kind} != {want}"
        if first:
            waits = ((sem, sem.n),) + tuple(waits)
        self.dma_open[eng][id(sem)] = sem
        if slow:
            self.emit(eng, (lambda h, o=out, i=in_: h.dma_start(
                out=o, in_=i, allow_slow_non_contiguous=True)), waits, incs)
        else:
            self.emit(eng, (lambda h, o=out, i=in_: h.dma_start(out=o, in_=i)),
                      waits, incs)

    def barrier(self):
        b = self.sem("bar")
        for e in ENGINES:
            waits = [(s, n) for (s, n) in self.eng_incs[e].values()]
            self.eng_incs[e] = {}
            if e in self.dma_open:
                waits += [(s, s.n) for s in self.dma_open[e].values()]
                self.dma_open[e] = {}
            self.emit(e, (lambda h: h.nop()), waits, ((b, 1),))
        tgt = b.n
        for e in ENGINES:
            self.emit(e, None, ((b, tgt),), ())

    def replay(self, block):
        for ename in ENGINES:
            ops = self.streams[ename]
            if not ops:
                continue

            def make(ops):
                def body(h):
                    maxw = {}
                    for fn, waits, incs in ops:
                        for s, v in waits:
                            if maxw.get(id(s), -1) >= v:
                                continue
                            h.wait_ge(s.h, v)
                            maxw[id(s)] = v
                        if fn is None:
                            for s, amt in incs:
                                h.nop().then_inc(s.h, amt)
                            continue
                        ins = fn(h)
                        for s, amt in incs:
                            ins.then_inc(s.h, amt)
                return body
            getattr(block, ename)(make(ops))


class Builder:
    def __init__(self, S=4096):
        self.S = S
        self.NKT = S // 512
        self.NKTA = self.NKT // 2
        self.SH = S // 2
        self.NQ = S // 2
        self.NQB = self.NQ // 512
        self.CP = max(1, self.NQ // 1024)
        self.TP = self.NQ // self.CP
        assert self.TP % 512 == 0

    def nkt_of(self, n):
        return 2 * n + 2

    def build(self):
        import contextlib
        nc = bass.Bass("TRN2", target_bir_lowering=False, debug=False)
        self.nc = nc
        S, NQ = self.S, self.NQ

        dt = lambda n, sh, ty, k="ExternalInput": nc.dram_tensor(n, sh, ty, kind=k).ap()
        self.xkv = dt("xkv", [self.SH, D], F32)
        self.xq = dt("xq", [NQ, D], F32)
        self.wdq = dt("wdq", [D, QP], F32)
        self.wqcat = dt("wqcat", [QP, QD], F32)
        self.wdkv = dt("wdkv", [D, KV], F32)
        self.wukv = dt("wukv", [KV, 3840], F32)
        self.wkr = dt("wkr", [D, 256], F32)
        self.wo = dt("wo", [D, D], F32)
        self.wg8 = dt("wg8", [D, INTER], I8)
        self.wu8 = dt("wu8", [D, INTER], I8)
        self.wd8 = dt("wd8", [INTER, D], I8)
        self.sgt = dt("sgt", [128, 64], F32)
        self.sut = dt("sut", [128, 64], F32)
        self.sdb = dt("sdb", [128, D], F32)
        self.qlw = dt("qlw", [128, QP], F32)
        self.qlb = dt("qlb", [128, QP], F32)
        self.kvlw = dt("kvlw", [128, KV], F32)
        self.kvlb = dt("kvlb", [128, KV], F32)
        self.coskT = dt("coskT", [128, self.SH], F32)
        self.sinkT = dt("sinkT", [128, self.SH], F32)
        self.cosqT = dt("cosqT", [128, NQ], F32)
        self.sinqT = dt("sinqT", [128, NQ], F32)
        self.maskst = dt("maskst", [128, 4096], BF16)
        self.identb = dt("identb", [128, 128], BF16)
        self.identf = dt("identf", [128, 128], F32)
        self.ktsh = dt("ktsh", [NH, 128, self.SH], BF16, "Internal")
        self.kts2g = [dt(f"kts2g{i}", [2, 4, 128, self.SH], BF16, "Internal")
                      for i in range(4)]
        self.vrowsh = dt("vrowsh", [self.SH, D], BF16, "Internal")
        self.vrows2 = dt("vrows2", [2, self.SH, D], BF16, "Internal")
        self.wots = dt("wots", [D, D], BF16, "Internal")
        self.ckvts = dt("ckvts", [KV, self.SH], BF16, "Internal")
        self.x1s = dt("x1s", [NQ, D], F32, "Internal")
        self.x1ts = dt("x1ts", [D, NQ], BF16, "Internal")
        self.kmsh = dt("kmsh", [1, NH * self.NKTA], F32, "Internal")
        self.kms2 = dt("kms2", [2, NH * self.NKTA], F32, "Internal")
        self.out = dt("out", [NQ, D], F32, "ExternalOutput")

        with contextlib.ExitStack() as stack:
            ctx = Ctx(nc, stack)
            self.ctx = ctx
            self.stack = stack
            sb = lambda n, sh, ty: stack.enter_context(nc.sbuf_tensor(n, sh, ty))
            self.ib = sb("ib", [128, 128], BF16)
            self.iff = sb("iff", [128, 128], F32)
            self.ones = sb("ones", [1, 128], F32)
            self.ones_cb = sb("ones_cb", [128, 1], BF16)
            self.krow = sb("krow", [1, 520], F32)
            self.kmxp = sb("kmxp", [1, NH * self.NKTA], F32)
            self.stats = sb("stats", [128, 16], F32)
            self.lcol = sb("lcol", [128, 4], F32)
            self.rlc = sb("rlc", [128, 4], F32)
            self.rls = sb("rls", [1, 512], F32)
            self.junkA = sb("junkA", [128, D], F32)
            self.junkB = sb("junkB", [128, 512], F32)
            self.junkC = sb("junkC", [128, 512], F32)
            self.sgt_sb = sb("sgt_sb", [128, 64], F32)
            self.sut_sb = sb("sut_sb", [128, 64], F32)
            self.eps_sb = sb("eps_sb", [128, 1], F32)

            self.pq = [stack.enter_context(nc.psum_tensor(f"pq{i}", [128, 1024], F32))
                       for i in range(4)]
            self.pb = [self.pq[i // 2][:][:, (i % 2) * 512:(i % 2) * 512 + 512]
                       for i in range(8)]
            self.pg = [Gate(ctx, f"pb{i}") for i in range(8)]

            g_pre = Gate(ctx, "pre")
            for fi, (out_sb, srcd) in enumerate(
                    ((self.ib, self.identb), (self.iff, self.identf),
                     (self.sgt_sb, self.sgt), (self.sut_sb, self.sut))):
                ctx.dma("sync", out_sb[:], srcd, (), g_pre.pi(16), first=fi == 0)
            g_init = Gate(ctx, "init")
            ctx.emit("vector", (lambda h: h.memset(self.ones[:], 1.0)), (),
                     g_init.pi())
            ctx.emit("vector", (lambda h: h.memset(self.ones_cb[:], 1.0)), (),
                     g_init.pi())
            ctx.emit("vector", (lambda h: h.memset(self.eps_sb[:], EPS)), (),
                     g_init.pi())
            ctx.emit("tensor", None, list(g_pre.cw()) + list(g_init.cw()), ())
            ctx.emit("scalar", None, list(g_pre.cw()) + list(g_init.cw()), ())
            ctx.emit("vector", None, list(g_pre.cw()), ())

            self.cc_gate = Gate(ctx, "ccd")
            for ph in (self.phase_wot, self.phase_a1, self.phase_a2,
                       self.phase_b, self.phase_c):
                ctx.push_scope()
                with contextlib.ExitStack() as pst:
                    self.phstack = pst
                    ph()
                ctx.barrier()
                ctx.pop_scope()

            ctx.emit("sync", None,
                     ((self.out_gate.s, self.out_gate.s.n),), ())

            with nc.Block() as block:
                ctx.replay(block)
        return nc

    def mm(self, pview, lhsT, rhs, start, stop, waits=(), incs=()):
        self.ctx.emit("tensor",
                      (lambda h, o=pview, a=lhsT, b=rhs, s1=start, s2=stop:
                       h.matmul(o, a, b, start=s1, stop=s2)), waits, incs)

    def tr(self, pview, inp, ident, waits=(), incs=()):
        self.ctx.emit("tensor",
                      (lambda h, o=pview, i=inp, d=ident: h.transpose(o, i, d)),
                      waits, incs)

    def ln_chain(self, pre_view, dim, out_view, gvs, gsv, gdone,
                 gamma=None, beta=None, out_tmp=None, pre_gate=None,
                 pre_ready=(), stats_base=0, junk=None):
        c = self.ctx
        st = self.stats
        if junk is None:
            junk = self.junkA
        msum, negm, sumsq, sdv, istd = (
            st[:, stats_base + i:stats_base + i + 1] for i in range(5))
        c.emit("vector", (lambda h, o=msum, i=pre_view:
                          h.reduce_sum(o, i, axis=AX.X)),
               tuple(pre_ready) + gvs.cw() + gdone.cw(), gvs.pi())
        c.emit("vector", (lambda h, o=negm, i=msum, k=-1.0 / dim:
                          h.tensor_scalar_mul(o, i, k)), gvs.cw(), gvs.pi())
        c.emit("scalar", (lambda h, o=junk[:, :dim], i=pre_view, b=negm,
                          a=sumsq:
                          h.activation(out=o, in_=i, func=AF.Square, bias=b,
                                       scale=1.0, accum_out=a)),
               gvs.cw() + gsv.cw(), gsv.pi())
        c.emit("scalar", (lambda h, o=sdv, i=sumsq, k=1.0 / dim:
                          h.activation(out=o, in_=i, func=AF.Sqrt,
                                       bias=self.eps_sb[:], scale=k)),
               gsv.cw(), gsv.pi())
        c.emit("vector", (lambda h, o=istd, i=sdv: h.reciprocal(o, i)),
               gsv.cw() + gdone.cw(), gdone.pi())
        if gamma is None:
            c.emit("vector", (lambda h, o=out_view, i=pre_view, a=negm, b=istd:
                              h.tensor_scalar(o, i, a, b, op0=ALU.add,
                                              op1=ALU.mult)),
                   gdone.cw(), gdone.pi())
        else:
            z = out_tmp
            c.emit("vector", (lambda h, o=z, i=pre_view, a=negm, b=istd:
                              h.tensor_scalar(o, i, a, b, op0=ALU.add,
                                              op1=ALU.mult)),
                   gdone.cw(), gdone.pi())
            c.emit("vector", (lambda h, o=z, i=z, g=gamma:
                              h.tensor_mul(o, i, g)), gdone.cw(), gdone.pi())
            c.emit("vector", (lambda h, o=out_view, i=z, g=beta:
                              h.tensor_add(o, i, g)), gdone.cw(), gdone.pi())
        if pre_gate is not None:
            pre_gate.freed(gdone)

    def knorm_chain(self, src_view, src_wait, src_freed_gate, h_, kt,
                    ksq, g_kn, pbank_gate, pbank_view):
        c = self.ctx
        NKT = self.NKT
        c.emit("scalar", (lambda h, o=ksq[:], i=src_view:
                          h.activation(out=o, in_=i, func=AF.Square)),
               tuple(src_wait) + g_kn.cw(), g_kn.pi())
        if src_freed_gate is not None:
            src_freed_gate.freed(g_kn)
        self.mm(pbank_view[0:1, 0:512], self.ones_cb[:], ksq[:],
                True, True, list(pbank_gate.pw()) + list(g_kn.cw()),
                pbank_gate.pi())
        c.emit("scalar", (lambda h, o=self.krow[0:1, 0:512],
                          i=pbank_view[0:1, 0:512]: h.copy(o, i)),
               list(pbank_gate.cw()) + list(g_kn.cw()), g_kn.pi())
        pbank_gate.freed(g_kn)
        c.emit("vector", (lambda h, o=self.krow[0:1, 512:513],
                          i=self.krow[0:1, 0:512]:
                          h.reduce_max(o, i, axis=AX.X)),
               g_kn.cw(), g_kn.pi())
        col = h_ * NKT + kt
        if kt == 0:
            c.emit("vector", (lambda h, o=self.kmxp[0:1, col:col + 1],
                              i=self.krow[0:1, 512:513]:
                              h.tensor_copy(o, i)), g_kn.cw(), g_kn.pi())
        else:
            c.emit("vector", (lambda h, o=self.kmxp[0:1, col:col + 1],
                              a=self.kmxp[0:1, col - 1:col],
                              b=self.krow[0:1, 512:513]:
                              h.tensor_max(o, a, b)), g_kn.cw(), g_kn.pi())

    def phase_wot(self):
        c = self.ctx
        nc = self.nc
        sb = lambda n, sh, ty: self.phstack.enter_context(nc.sbuf_tensor(n, sh, ty))
        wob = [sb(f"wob{i}", [128, D], BF16) for i in range(4)]
        wost = [sb(f"wost{i}", [128, 512], BF16) for i in range(2)]
        g_ld = [Gate(c, f"wob{i}", kind="g") for i in range(4)]
        g_st = [Gate(c, f"wost{i}") for i in range(2)]
        g_wd = Gate(c, "wotd")
        bank = self.pg[0]
        bt = self.pb[0].bitcast(BF16)
        for rcb in range(4):
            for k in range(4):
                rc = rcb * 4 + k
                c.dma("gpsimd", wob[k][:], self.wo[rc * 128:(rc + 1) * 128, :],
                      g_ld[k].pw(), g_ld[k].pi(16))
            for jc in range(16):
                si = (rcb * 16 + jc) % 2
                for k in range(4):
                    w = []
                    if k == 0:
                        w = list(bank.pw())
                        if jc == 0:
                            for kk in range(4):
                                w += list(g_ld[kk].cw())
                    self.tr(bt[:, k * 128:(k + 1) * 128],
                            wob[k][:, jc * 128:(jc + 1) * 128], self.ib[:],
                            w, bank.pi() if k == 3 else ())
                c.emit("scalar", (lambda h, o=wost[si][:], i=bt[:, 0:512]:
                                  h.copy(o, i)),
                       list(bank.cw()) + list(g_st[si].pw()), g_st[si].pi())
                bank.freed(g_st[si])
                if jc == 15:
                    for kk in range(4):
                        g_ld[kk].freed(g_st[si])
                c.dma("sync", self.wots[jc * 128:(jc + 1) * 128,
                                        rcb * 512:(rcb + 1) * 512],
                      wost[si][:], g_st[si].cw(), g_wd.pi(16))
                g_st[si].freed(g_wd)

    def phase_a1(self):
        c = self.ctx
        nc = self.nc
        NKT = self.NKT
        sb = lambda n, sh, ty: self.phstack.enter_context(nc.sbuf_tensor(n, sh, ty))
        wdkv_sb = sb("wdkv_sb", [128, 16 * KV], BF16)
        wkr_sb = sb("wkr_sb", [128, 16 * 256], BF16)
        xa = [sb(f"xa{i}", [128, D], F32) for i in range(2)]
        xab = [sb(f"xab{i}", [128, D], BF16) for i in range(2)]
        xT = sb("xT", [128, 16 * 512], BF16)
        ckvpre = [sb(f"ckvpre{i}", [128, KV], F32) for i in range(2)]
        ckvbf = [sb(f"ckvbf{i}", [128, KV], BF16) for i in range(2)]
        ckvTst = [sb(f"ckvTst{i}", [128, 512], BF16) for i in range(2)]
        ck_sb = sb("ck_sb", [128, 512], F32)
        sk_sb = sb("sk_sb", [128, 512], F32)
        krf = sb("krf", [128, 512], F32)
        krrot = sb("krrot", [128, 512], F32)
        krb = sb("krb", [128, 512], BF16)
        ksq1 = [sb(f"ksq1{i}", [128, 512], BF16) for i in range(2)]
        junk2 = sb("junk2", [128, KV], F32)
        kvlw_sb = sb("kvlw_sb", [128, KV], F32)
        kvlb_sb = sb("kvlb_sb", [128, KV], F32)

        g_w = Gate(c, "a1.w", kind="g")
        g_xa = [Gate(c, f"a1.xa{i}") for i in range(2)]
        g_xab = [Gate(c, f"a1.xab{i}") for i in range(2)]
        g_xT = Gate(c, "a1.xT")
        g_pre = [Gate(c, f"a1.pre{i}") for i in range(2)]
        g_ckv = [Gate(c, f"a1.ckv{i}") for i in range(2)]
        g_cst = [Gate(c, f"a1.cst{i}") for i in range(2)]
        g_cd = Gate(c, "a1.cd")
        g_rope = Gate(c, "a1.rope")
        g_krf = Gate(c, "a1.krf")
        g_krb = Gate(c, "a1.krb")
        g_kd = Gate(c, "a1.kd")
        g_kn1 = Gate(c, "a1.kn1")
        g_lvs2 = [Gate(c, f"a1.lvs{i}") for i in range(2)]
        g_lsv2 = [Gate(c, f"a1.lsv{i}") for i in range(2)]

        for dc in range(16):
            c.dma("gpsimd", wdkv_sb[:, dc * KV:(dc + 1) * KV],
                  self.wdkv[dc * 128:(dc + 1) * 128, :], (), g_w.pi(16),
                  first=dc == 0)
            c.dma("gpsimd", wkr_sb[:, dc * 256:(dc + 1) * 256],
                  self.wkr[dc * 128:(dc + 1) * 128, :], (), g_w.pi(16),
                  first=False)
        g_wl = Gate(c, "a1.wl")
        c.dma("sync", kvlw_sb[:], self.kvlw, (), g_wl.pi(16))
        c.dma("sync", kvlb_sb[:], self.kvlb, (), g_wl.pi(16), first=False)
        wready = list(g_w.cw()) + list(g_wl.cw())

        pts = self.pg[:6]
        trp = [self.pg[6], self.pg[7]]
        btr = [self.pb[6].bitcast(BF16), self.pb[7].bitcast(BF16)]
        trp4 = [self.pg[i] for i in range(4)]
        btr4 = [self.pb[i].bitcast(BF16) for i in range(4)]

        krn = [0]
        kr_pend = []
        kr_sqw = [None, None]

        def a1_krow(h_, kt_, oc_, sqw):
            ri = krn[0]
            krn[0] += 1
            pbg = self.pg[4 + ri % 2]
            pbv = self.pb[4 + ri % 2]
            self.mm(pbv[0:1, 0:512], self.ones_cb[:], ksq1[oc_][:],
                    True, True, list(pbg.pw()) + [sqw], pbg.pi())
            c.emit("scalar", (lambda h, o=self.krow[0:1, 0:512],
                              i=pbv[0:1, 0:512]: h.copy(o, i)),
                   list(pbg.cw()) + list(g_kn1.cw()), g_kn1.pi())
            pbg.freed(g_kn1)
            col = h_ * self.NKTA + kt_
            c.emit("vector", (lambda h, o=self.kmxp[0:1, col:col + 1],
                              i=self.krow[0:1, 0:512]:
                              h.reduce_max(o, i, axis=AX.X)),
                   g_kn1.cw(), g_kn1.pi())

        for kt in range(self.NKTA):
            c.dma("sync", ck_sb[:], self.coskT[:, kt * 512:(kt + 1) * 512],
                  g_rope.pw(), g_rope.pi(16))
            c.dma("sync", sk_sb[:], self.sinkT[:, kt * 512:(kt + 1) * 512],
                  (), g_rope.pi(16), first=False)
            for ch in range(4):
                if ch == 1 and kr_pend:
                    a1_krow(*kr_pend.pop(0))
                i = ch % 2
                c.dma("sync", xa[i][:],
                      self.xkv[kt * 512 + ch * 128: kt * 512 + (ch + 1) * 128, :],
                      g_xa[i].pw(), g_xa[i].pi(16))
                xabr = []
                for qb in range(4):
                    c.emit("vector", (lambda h,
                                      o=xab[i][:, qb * 512:(qb + 1) * 512],
                                      s=xa[i][:, qb * 512:(qb + 1) * 512]:
                                      h.tensor_copy(o, s)),
                           list(g_xa[i].cw()) +
                           (list(g_xab[i].pw()) if qb == 0 else []),
                           g_xab[i].pi())
                    xabr.append((g_xab[i].s, g_xab[i].s.n))
                g_xa[i].freed(g_xab[i])
                for db in range(4):
                    tg = trp4[db]
                    tv = btr4[db]
                    for k in range(4):
                        dc = db * 4 + k
                        w = []
                        if k == 0:
                            w = list(tg.pw()) + [xabr[db]]
                            if ch == 0:
                                w += list(g_xT.pw())
                        self.tr(tv[:, k * 128:(k + 1) * 128],
                                xab[i][:, dc * 128:(dc + 1) * 128], self.ib[:],
                                w, tg.pi() if k == 3 else ())
                    dst = xT[:].rearrange("p (dc t) -> p dc t", dc=16)[
                        :, db * 4:(db + 1) * 4, ch * 128:(ch + 1) * 128]
                    src = tv[:, 0:512].rearrange("p (b t) -> p b t", b=4)
                    c.emit("scalar", (lambda h, o=dst, s=src: h.copy(o, s)),
                           tg.cw(), g_xT.pi())
                    tg.freed(g_xT)
                    if db == 3:
                        g_xab[i].freed(g_xT)
            xtread = g_xT.cw()

            def a1_mms(ts):
                bset = pts[3 * (ts % 2): 3 * (ts % 2) + 3]
                for dc in range(16):
                    lhsT = xT[:, dc * 512 + ts * 128: dc * 512 + (ts + 1) * 128]
                    for si, (o0, o1) in enumerate(((0, 512), (512, 1024), (1024, KV))):
                        w = []
                        if dc == 0:
                            w = list(bset[si].pw())
                            if si == 0:
                                w += list(xtread) + list(wready)
                        self.mm(self.pb[3 * (ts % 2) + si][:, 0:o1 - o0],
                                lhsT,
                                wdkv_sb[:, dc * KV + o0: dc * KV + o1],
                                dc == 0, dc == 15, w,
                                bset[si].pi() if dc == 15 else ())

            def a1_post(ts, kt):
                ci = ts % 2
                bset = pts[3 * ci: 3 * ci + 3]
                for si, (o0, o1) in enumerate(((0, 512), (512, 1024), (1024, KV))):
                    bk = bset[si]
                    c.emit("scalar",
                           (lambda h, o=ckvpre[ci][:, o0:o1],
                            i=self.pb[3 * ci + si][:, 0:o1 - o0]:
                            h.copy(o, i)),
                           list(bk.cw()) + (list(g_pre[ci].pw()) if si == 0 else []),
                           g_pre[ci].pi())
                    bk.freed(g_pre[ci])
                c.emit("vector", None, list(g_ckv[ci].pw()), ())
                jn = self.junkA[:, 0:KV] if ci == 0 else junk2[:, 0:KV]
                self.ln_chain(ckvpre[ci][:], KV, ckvbf[ci][:], g_lvs2[ci],
                              g_lsv2[ci], g_ckv[ci], kvlw_sb[:], kvlb_sb[:],
                              jn, pre_gate=g_pre[ci],
                              pre_ready=g_pre[ci].cw(), stats_base=5 * ci,
                              junk=jn)
                for bi, grp in enumerate(((0, 1, 2, 3), (4, 5, 6, 7), (8, 9), (10,))):
                    si = bi % 2
                    for k, cc in enumerate(grp):
                        rows = kvrows(cc)
                        w = []
                        if k == 0:
                            w = list(trp[si].pw()) + list(g_ckv[ci].cw())
                        self.tr(btr[si][0:rows, k * 128:(k + 1) * 128],
                                ckvbf[ci][:, cc * 128:cc * 128 + rows],
                                self.ib[:],
                                w, trp[si].pi() if k == len(grp) - 1 else ())
                    rows = kvrows(grp[-1])
                    wid = 128 * len(grp)
                    c.emit("scalar",
                           (lambda h, o=ckvTst[si][0:128, 0:wid],
                            i=btr[si][:, 0:wid]: h.copy(o, i)),
                           list(trp[si].cw()) + list(g_cst[si].pw()),
                           g_cst[si].pi())
                    trp[si].freed(g_cst[si])
                    if bi == 3:
                        g_ckv[ci].freed(g_cst[si])
                    r0 = grp[0] * 128
                    col = kt * 512 + ts * 128
                    if rows != 128:
                        src = ckvTst[si][0:rows, 0:128]
                        dst = self.ckvts[r0:r0 + rows, col:col + 128]
                    else:
                        src = ckvTst[si][:].rearrange(
                            "p (b t) -> p b t", b=4)[:, 0:len(grp), 0:128]
                        dst = self.ckvts[r0:r0 + 128 * len(grp),
                                         col:col + 128].rearrange(
                            "(b p) t -> p b t", p=128)
                    c.dma("sync", dst, src, g_cst[si].cw(), g_cd.pi(16))
                    g_cst[si].freed(g_cd)

            for stage in range(5):
                if stage < 4:
                    a1_mms(stage)
                if stage >= 1:
                    a1_post(stage - 1, kt)
            for oc in range(2):
                bk = pts[oc]
                for dc in range(16):
                    lhsT = wkr_sb[:, dc * 256 + oc * 128: dc * 256 + (oc + 1) * 128]
                    w = list(bk.pw()) if dc == 0 else []
                    self.mm(self.pb[oc][:, 0:512], lhsT,
                            xT[:, dc * 512:(dc + 1) * 512],
                            dc == 0, dc == 15, w, bk.pi() if dc == 15 else ())
                if oc == 1:
                    g_xT.freed(bk)
                c.emit("scalar", (lambda h, o=krf[:], i=self.pb[oc][:, 0:512]:
                                  h.copy(o, i)),
                       list(bk.cw()) + list(g_krf.pw()), g_krf.pi())
                c.emit("scalar", (lambda h, o=ksq1[oc][:], i=krf[:]:
                                  h.activation(out=o, in_=i, func=AF.Square)),
                       g_krf.cw() + g_kn1.cw(), g_kn1.pi())
                kr_sqw[oc] = (g_kn1.s, g_kn1.s.n)
                g_krf.freed(g_kn1)
                c.emit("scalar", (lambda h, o=krrot[0:64, :], i=krf[64:128, :]:
                                  h.activation(out=o, in_=i, func=AF.Copy,
                                               scale=-1.0)),
                       g_krf.cw(), g_krf.pi())
                c.emit("scalar", (lambda h, o=krrot[64:128, :], i=krf[0:64, :]:
                                  h.copy(o, i)), g_krf.cw(), g_krf.pi())
                bk.freed(g_krf)
                if oc == 1:
                    a1_krow(14, kt, 0, kr_sqw[0])
                    kr_pend.append((15, kt, 1, kr_sqw[1]))
                c.emit("vector", (lambda h, o=self.junkB[:], a=krf[:], b=ck_sb[:]:
                                  h.tensor_mul(o, a, b)),
                       list(g_krf.cw()) + list(g_rope.cw()) + list(g_krb.pw()) +
                       list(g_krb.cw()), g_krb.pi())
                c.emit("vector", (lambda h, o=self.junkC[:], a=krrot[:], b=sk_sb[:]:
                                  h.tensor_mul(o, a, b)), g_krb.cw(), g_krb.pi())
                c.emit("vector", (lambda h, o=krb[:], a=self.junkB[:],
                                  b=self.junkC[:]: h.tensor_add(o, a, b)),
                       g_krb.cw(), g_krb.pi())
                g_krf.freed(g_krb)
                if oc == 1:
                    g_rope.freed(g_krb)
                c.dma("sync", self.ktsh[14 + oc, :, kt * 512:(kt + 1) * 512],
                      krb[:], g_krb.cw(), g_kd.pi(16))
                g_krb.freed(g_kd)
        while kr_pend:
            a1_krow(*kr_pend.pop(0))

    def phase_a2(self):
        c = self.ctx
        nc = self.nc
        NKT = self.NKT
        sb = lambda n, sh, ty: self.phstack.enter_context(nc.sbuf_tensor(n, sh, ty))
        wuk_sb = sb("wuk_sb", [128, KVC * 2048], BF16)
        wukK_sb = sb("wukK_sb", [128, KVC * 1792], BF16)
        ckvT_sb = [sb(f"ckvT_sb{i}", [128, KVC * 512], BF16) for i in range(2)]
        evst = [sb(f"evst{i}", [128, 512], BF16) for i in range(2)]
        ksq2 = sb("ksq2", [128, 512], BF16)
        g_wu = Gate(c, "a2.wu", kind="g")
        g_wuK = Gate(c, "a2.wuK", kind="g")
        g_kn2 = Gate(c, "a2.kn2")
        g_ct = [Gate(c, f"a2.ct{i}") for i in range(2)]
        g_ev = [Gate(c, f"a2.ev{i}") for i in range(2)]
        g_kvd = Gate(c, "a2.kvd")

        def load_ckvT(kt, ti):
            for cc in range(KVC):
                rows = kvrows(cc)
                c.dma("sync", ckvT_sb[ti][0:rows, cc * 512:(cc + 1) * 512],
                      self.ckvts[cc * 128:cc * 128 + rows,
                                 kt * 512:(kt + 1) * 512],
                      g_ct[ti].pw() if cc == 0 else (), g_ct[ti].pi(16),
                      first=cc == 0)

        ksqs = [ksq2, self.phstack.enter_context(
            nc.sbuf_tensor("ksq2b", [128, 512], BF16))]
        pend_kn = []

        def knorm_sq(si2, src_view, src_wait, src_freed_gate):
            c.emit("scalar", (lambda h, o=ksqs[si2][:], i=src_view:
                              h.activation(out=o, in_=i, func=AF.Square)),
                   tuple(src_wait) + g_kn2.cw(), g_kn2.pi())
            src_freed_gate.freed(g_kn2)
            return (g_kn2.s, g_kn2.s.n)

        def knorm_row(h_, kt_, si2, ri, sqw):
            pbg = self.pg[4 + ri % 2]
            pbv = self.pb[4 + ri % 2]
            self.mm(pbv[0:1, 0:512], self.ones_cb[:], ksqs[si2][:],
                    True, True, list(pbg.pw()) + [sqw],
                    pbg.pi())
            c.emit("scalar", (lambda h, o=self.krow[0:1, 0:512],
                              i=pbv[0:1, 0:512]: h.copy(o, i)),
                   list(pbg.cw()) + list(g_kn2.cw()), g_kn2.pi())
            pbg.freed(g_kn2)
            col = h_ * self.NKTA + kt_
            c.emit("vector", (lambda h, o=self.kmxp[0:1, col:col + 1],
                              i=self.krow[0:1, 0:512]:
                              h.reduce_max(o, i, axis=AX.X)),
                   g_kn2.cw(), g_kn2.pi())

        groups = [[0, 1], [2, 3], [4, 5], [6, 7]]
        g_cc = self.cc_gate

        for cc in range(KVC):
            rows = kvrows(cc)
            c.dma("gpsimd", wuk_sb[0:rows, cc * 2048:(cc + 1) * 2048],
                  self.wukv[cc * 128:cc * 128 + rows, 1792:3840],
                  g_wu.pw() if cc == 0 else (), g_wu.pi(16), first=cc == 0)
        wur = g_wu.cw()
        for cc in range(KVC):
            rows = kvrows(cc)
            c.dma("gpsimd", wukK_sb[0:rows, cc * 1792:(cc + 1) * 1792],
                  self.wukv[cc * 128:cc * 128 + rows, 0:1792],
                  g_wuK.pw() if cc == 0 else (), g_wuK.pi(16), first=cc == 0)
        wurK = g_wuK.cw()
        for kt in range(self.NKTA):
            ti = kt % 2
            load_ckvT(kt, ti)
            for g in range(16):
                kc, vg = divmod(g, 4)
                bk = self.pg[g % 4]
                for cc in range(KVC):
                    rows = kvrows(cc)
                    w = []
                    if cc == 0:
                        w = list(bk.pw()) + list(g_ct[ti].cw()) + list(wur)
                    self.mm(self.pb[g % 4][:, 0:512],
                            ckvT_sb[ti][0:rows, cc * 512 + kc * 128:
                                        cc * 512 + (kc + 1) * 128],
                            wuk_sb[0:rows, cc * 2048 + vg * 512:
                                   cc * 2048 + (vg + 1) * 512],
                            cc == 0, cc == KVC - 1, w,
                            bk.pi() if cc == KVC - 1 else ())
                if g == 15:
                    g_ct[ti].freed(bk)
                    if kt == self.NKTA - 1:
                        g_wu.freed(bk)
                si = g % 2
                c.emit("scalar", (lambda h, o=evst[si][:],
                                  i=self.pb[g % 4][:, 0:512]: h.copy(o, i)),
                       list(bk.cw()) + list(g_ev[si].pw()), g_ev[si].pi())
                bk.freed(g_ev[si])
                c.dma("sync",
                      self.vrowsh[kt * 512 + kc * 128: kt * 512 + (kc + 1) * 128,
                                 vg * 512:(vg + 1) * 512],
                      evst[si][:], g_ev[si].cw(), g_kvd.pi(16))
                g_ev[si].freed(g_kvd)
        c.emit("gpsimd",
               (lambda h, gr=groups:
                h.collective_compute("AllGather", ALU.bypass,
                                     replica_groups=gr,
                                     ins=[self.vrowsh], outs=[self.vrows2])),
               list(g_kvd.cw()), g_cc.pi(), track=False)
        self.ccw_v = [(g_cc.s, g_cc.s.n)]

        rown = 0
        for kt in range(self.NKTA):
            ti = kt % 2
            load_ckvT(kt, ti)
            for ocl in range(14):
                bk = self.pg[ocl % 4]
                for cc in range(KVC):
                    rows = kvrows(cc)
                    w = []
                    if cc == 0:
                        w = list(bk.pw()) + list(g_ct[ti].cw()) + list(wurK)
                    self.mm(self.pb[ocl % 4][:, 0:512],
                            wukK_sb[0:rows, cc * 1792 + ocl * 128:
                                    cc * 1792 + (ocl + 1) * 128],
                            ckvT_sb[ti][0:rows, cc * 512:(cc + 1) * 512],
                            cc == 0, cc == KVC - 1, w,
                            bk.pi() if cc == KVC - 1 else ())
                if ocl == 13:
                    g_ct[ti].freed(bk)
                    if kt == self.NKTA - 1:
                        g_wuK.freed(bk)
                si = ocl % 2
                c.emit("scalar", (lambda h, o=evst[si][:],
                                  i=self.pb[ocl % 4][:, 0:512]: h.copy(o, i)),
                       list(bk.cw()) + list(g_ev[si].pw()), g_ev[si].pi())
                bk.freed(g_ev[si])
                c.dma("sync", self.ktsh[ocl, :, kt * 512:(kt + 1) * 512],
                      evst[si][:], g_ev[si].cw(), g_kvd.pi(16))
                g_ev[si].freed(g_kvd)
                sqw = knorm_sq(ocl % 2, evst[si][:], g_ev[si].cw(), g_ev[si])
                if pend_kn:
                    h_p, kt_p, si_p, w_p = pend_kn.pop(0)
                    knorm_row(h_p, kt_p, si_p, rown, w_p)
                    rown += 1
                pend_kn.append((ocl, kt, ocl % 2, sqw))
        while pend_kn:
            h_p, kt_p, si_p, w_p = pend_kn.pop(0)
            knorm_row(h_p, kt_p, si_p, rown, w_p)
            rown += 1
        c.dma("sync", self.kmsh, self.kmxp[:], g_kn2.cw(), g_kn2.pi(16))

    def emit_cc_kms(self):
        c = self.ctx
        groups = [[0, 1], [2, 3], [4, 5], [6, 7]]
        g_cc = self.cc_gate
        c.emit("gpsimd",
               (lambda h, gr=groups:
                h.collective_compute("AllGather", ALU.bypass,
                                     replica_groups=gr,
                                     ins=[self.kmsh], outs=[self.kms2])),
               (), g_cc.pi(), track=False)
        self.ccw_m = [(g_cc.s, g_cc.s.n)]
        self.ccw_g = []

    def emit_cc_kts(self, gi):
        c = self.ctx
        groups = [[0, 1], [2, 3], [4, 5], [6, 7]]
        g_cc = self.cc_gate
        c.emit("gpsimd",
               (lambda h, gr=groups,
                i=self.ktsh[gi * 4:(gi + 1) * 4],
                o=self.kts2g[gi]:
                h.collective_compute("AllGather", ALU.bypass,
                                     replica_groups=gr,
                                     ins=[i], outs=[o])),
               (), g_cc.pi(), track=False)
        self.ccw_g.append([(g_cc.s, g_cc.s.n)])

    def phase_b(self):
        c = self.ctx
        nc = self.nc
        NQB = self.NQB
        sb = lambda n, sh, ty: self.phstack.enter_context(nc.sbuf_tensor(n, sh, ty))
        qlw_sb = sb("qlw_sb", [128, QP], F32)
        qlb_sb = sb("qlb_sb", [128, QP], F32)
        xa2 = [sb(f"xa2_{i}", [128, D], F32) for i in range(2)]
        xb2 = sb("xb2", [128, D], BF16)
        xTq = sb("xTq", [128, 16 * 512], BF16)
        wstr = [sb(f"wstr{i}", [128, 1024], BF16) for i in range(2)]
        cqf = sb("cqf", [128, QP], F32)
        cqbf = sb("cqbf", [128, QP], BF16)
        cqT = sb("cqT", [128, 8 * 512], BF16)
        QT = sb("QT", [128, 16 * 512], BF16)
        U = sb("U", [128, 69632], U8)
        ub = U[:].bitcast(BF16)
        uf = U[:].bitcast(F32)
        kh = [ub[:, 0:4096], ub[:, 4096:8192]]
        vh = [ub[:, 8192:12288], ub[:, 12288:16384]]
        pts4 = [ub[:, 24576 + 512 * i:25088 + 512 * i] for i in range(4)]
        mbt = ub[:, 26624:30720]
        x1pre = uf[:, 0:8192]
        xqr = [uf[:, 8192:10240], uf[:, 10240:12288]]
        qrf = uf[:, 15360:15872]
        qrrot = uf[:, 15872:16384]
        cq_ct = uf[:, 16384:16896]
        cq_st = uf[:, 16896:17408]
        rlbc = sb("rlbc", [128, 512], F32)
        kmxB = self.phstack.enter_context(
            nc.sbuf_tensor("kmxB", [1, NH * self.NKT], F32))
        nch_sb = sb("nch_sb", [128, NH], F32)
        junkQ = sb("junkQ", [128, 512], BF16)
        lrow_sb = self.phstack.enter_context(
            nc.sbuf_tensor("lrow_sb", [1, 1024], F32))
        aoT = sb("aoT", [128, 16 * 512], BF16)
        wotr = [sb(f"wotr{i}", [128, 1024], BF16) for i in range(2)]
        x1Tst = [sb(f"x1Tst{i}", [128, 512], BF16) for i in range(2)]

        self.emit_cc_kms()
        self.emit_cc_kts(0)

        g_pre = Gate(c, "b.pre")
        c.dma("sync", mbt, self.maskst, (), g_pre.pi(16))
        c.dma("sync", qlw_sb[:], self.qlw, (), g_pre.pi(16), first=False)
        c.dma("sync", qlb_sb[:], self.qlb, (), g_pre.pi(16), first=False)
        NKTA = self.NKTA
        for half in range(2):
            dst = kmxB[0:1, :].rearrange("o (h k) -> o h k", k=self.NKT)[
                :, :, half * NKTA:(half + 1) * NKTA]
            src = self.kms2[half:half + 1, :].rearrange(
                "o (h k) -> o h k", k=NKTA)
            c.dma("sync", dst, src, list(self.ccw_m) if half == 0 else (),
                  g_pre.pi(16), first=half == 0, slow=True)
        pre_r = g_pre.cw()

        g_xa = [Gate(c, f"b.xa{i}") for i in range(2)]
        g_xb = Gate(c, "b.xb")
        g_xTq = Gate(c, "b.xTq")
        g_ws = [Gate(c, f"b.ws{i}", kind="g") for i in range(2)]
        g_cqf = Gate(c, "b.cqf")
        g_cqbf = Gate(c, "b.cqbf")
        g_cqT = Gate(c, "b.cqT")
        g_QT = Gate(c, "b.QT")
        g_qrf = Gate(c, "b.qrf")
        g_rope = Gate(c, "b.rope")
        g_kh = [Gate(c, f"b.kh{i}", kind="g") for i in range(2)]
        g_vh = [Gate(c, f"b.vh{i}", kind="g") for i in range(2)]
        g_pt = [Gate(c, f"b.pt{i}") for i in range(4)]
        g_l6 = [Gate(c, f"b.l6{i}") for i in range(2)]
        g_nch = Gate(c, "b.nch")
        g_rls = Gate(c, "b.rls")
        g_rlb = Gate(c, "b.rlb")
        g_aoT = Gate(c, "b.aoT")
        g_wot = [Gate(c, f"b.wot{i}") for i in range(2)]
        g_x1p = Gate(c, "b.x1p")
        g_xqr = [Gate(c, f"b.xqr{i}") for i in range(2)]
        g_x1f = Gate(c, "b.x1f")
        g_x1T = [Gate(c, f"b.x1T{i}") for i in range(2)]
        g_x1d = Gate(c, "b.x1d")
        g_lvs = Gate(c, "b.lvs")
        g_lsv = Gate(c, "b.lsv")
        bendw = []
        prev_aor = ()
        g_dcp = Gate(c, "b.dcp")
        g_icp = Gate(c, "b.icp")
        g_avp = Gate(c, "b.avp")

        ptrb = self.pg[7]
        btr = self.pb[7].bitcast(BF16)

        for n in range(NQB):
            nkt = self.nkt_of(n)
            nk = nkt * 512
            nkc = nk // 128
            c.dma("sync", cq_ct, self.cosqT[:, n * 512:(n + 1) * 512],
                  list(g_rope.pw()) + list(prev_aor), g_rope.pi(16))
            c.dma("sync", cq_st, self.sinqT[:, n * 512:(n + 1) * 512],
                  (), g_rope.pi(16), first=False)
            for ch in range(4):
                i = ch % 2
                c.dma("sync", xa2[i][:],
                      self.xq[n * 512 + ch * 128: n * 512 + (ch + 1) * 128, :],
                      g_xa[i].pw(), g_xa[i].pi(16))
                c.emit("vector", (lambda h, o=xb2[:], s=xa2[i][:]:
                                  h.tensor_copy(o, s)),
                       list(g_xa[i].cw()) + list(g_xb.pw()), g_xb.pi())
                g_xa[i].freed(g_xb)
                for db in range(4):
                    for k in range(4):
                        dc = db * 4 + k
                        w = []
                        if k == 0:
                            w = list(ptrb.pw()) + list(g_xb.cw())
                            if ch == 0 and db == 0:
                                w += list(g_xTq.pw())
                        self.tr(btr[:, k * 128:(k + 1) * 128],
                                xb2[:, dc * 128:(dc + 1) * 128], self.ib[:],
                                w, ptrb.pi() if k == 3 else ())
                    dst = xTq[:].rearrange("p (dc t) -> p dc t", dc=16)[
                        :, db * 4:(db + 1) * 4, ch * 128:(ch + 1) * 128]
                    src = btr[:, 0:512].rearrange("p (b t) -> p b t", b=4)
                    c.emit("scalar", (lambda h, o=dst, s=src: h.copy(o, s)),
                           ptrb.cw(), g_xTq.pi())
                    ptrb.freed(g_xTq)
                    if db == 3:
                        g_xb.freed(g_xTq)
            xtr = g_xTq.cw()
            for dc in range(16):
                i = dc % 2
                c.dma("gpsimd", wstr[i][:],
                      self.wdq[dc * 128:(dc + 1) * 128, :],
                      g_ws[i].pw(), g_ws[i].pi(16))
                for ts in range(4):
                    lhsT = xTq[:, dc * 512 + ts * 128: dc * 512 + (ts + 1) * 128]
                    for si in range(2):
                        bk = self.pg[2 * ts + si]
                        w = []
                        if dc == 0:
                            w = list(bk.pw())
                            if ts == 0 and si == 0:
                                w += list(xtr) + list(pre_r)
                        if si == 0:
                            w = w + list(g_ws[i].cw())
                        last = ts == 3 and si == 1
                        inc = bk.pi() if dc == 15 else (g_dcp.pi() if last else ())
                        self.mm(self.pb[2 * ts + si][:, 0:512], lhsT,
                                wstr[i][:, si * 512:(si + 1) * 512],
                                dc == 0, dc == 15, w, inc)
                        if last:
                            g_ws[i].freed(g_dcp if dc < 15 else bk)
                            if dc == 15:
                                g_xTq.freed(bk)
            if n == 0:
                self.emit_cc_kts(1)
            for ts in range(4):
                for si in range(2):
                    bk = self.pg[2 * ts + si]
                    c.emit("scalar", (lambda h, o=cqf[:, si * 512:(si + 1) * 512],
                                      i=self.pb[2 * ts + si][:, 0:512]:
                                      h.copy(o, i)),
                           list(bk.cw()) + (list(g_cqf.pw()) if si == 0 else []),
                           g_cqf.pi())
                    bk.freed(g_cqf)
                c.emit("vector", None, list(g_cqbf.pw()), ())
                self.ln_chain(cqf[:], QP, cqbf[:], g_lvs, g_lsv, g_cqbf,
                              qlw_sb[:], qlb_sb[:], self.junkA[:, 0:QP],
                              pre_gate=g_cqf, pre_ready=g_cqf.cw())
                for db in range(2):
                    for k in range(4):
                        cc = db * 4 + k
                        w = []
                        if k == 0:
                            w = list(ptrb.pw()) + list(g_cqbf.cw())
                        self.tr(btr[:, k * 128:(k + 1) * 128],
                                cqbf[:, cc * 128:(cc + 1) * 128], self.ib[:],
                                w, ptrb.pi() if k == 3 else ())
                    dst = cqT[:].rearrange("p (cc t) -> p cc t", cc=8)[
                        :, db * 4:(db + 1) * 4, ts * 128:(ts + 1) * 128]
                    src = btr[:, 0:512].rearrange("p (b t) -> p b t", b=4)
                    c.emit("scalar", (lambda h, o=dst, s=src: h.copy(o, s)),
                           list(ptrb.cw()) +
                           (list(g_cqT.pw()) if ts == 0 and db == 0 else []),
                           g_cqT.pi())
                    ptrb.freed(g_cqT)
                    if db == 1:
                        g_cqbf.freed(g_cqT)
            cqtr = g_cqT.cw()
            for oh in range(2):
                for cqc in range(8):
                    i = cqc % 2
                    c.dma("gpsimd", wstr[i][:],
                          self.wqcat[cqc * 128:(cqc + 1) * 128,
                                     oh * 1024:(oh + 1) * 1024],
                          g_ws[i].pw(), g_ws[i].pi(16))
                    for ocl in range(8):
                        bk = self.pg[ocl]
                        w = []
                        if cqc == 0:
                            w = list(bk.pw())
                            if ocl == 0:
                                w += list(cqtr)
                        if ocl == 0:
                            w = w + list(g_ws[i].cw())
                        inc = bk.pi() if cqc == 7 else (g_dcp.pi() if ocl == 7 else ())
                        self.mm(self.pb[ocl][:, 0:512],
                                wstr[i][:, ocl * 128:(ocl + 1) * 128],
                                cqT[:, cqc * 512:(cqc + 1) * 512],
                                cqc == 0, cqc == 7, w, inc)
                        if ocl == 7:
                            g_ws[i].freed(g_dcp if cqc < 7 else bk)
                            if cqc == 7 and oh == 1:
                                g_cqT.freed(bk)
                for ocl in range(8):
                    oc = oh * 8 + ocl
                    bk = self.pg[ocl]
                    if oc < 14:
                        c.emit("scalar",
                               (lambda h, o=QT[:, oc * 512:(oc + 1) * 512],
                                i=self.pb[ocl][:, 0:512]:
                                h.activation(out=o, in_=i, func=AF.Copy,
                                             scale=SCALE)),
                               list(bk.cw()) +
                               (list(g_QT.pw()) if oc == 0 else []),
                               g_QT.pi())
                        bk.freed(g_QT)
                    else:
                        c.emit("scalar", (lambda h, o=qrf,
                                          i=self.pb[ocl][:, 0:512]:
                                          h.activation(out=o, in_=i, func=AF.Copy,
                                                       scale=SCALE)),
                               list(bk.cw()) + list(g_qrf.pw()), g_qrf.pi())
                        c.emit("scalar", (lambda h, o=qrrot[0:64, :],
                                          i=qrf[64:128, :]:
                                          h.activation(out=o, in_=i, func=AF.Copy,
                                                       scale=-1.0)),
                               g_qrf.cw(), g_qrf.pi())
                        c.emit("scalar", (lambda h, o=qrrot[64:128, :],
                                          i=qrf[0:64, :]:
                                          h.copy(o, i)), g_qrf.cw(), g_qrf.pi())
                        bk.freed(g_qrf)
                        c.emit("vector", (lambda h, o=self.junkB[:], a=qrf,
                                          b=cq_ct: h.tensor_mul(o, a, b)),
                               list(g_qrf.cw()) + list(g_rope.cw()) +
                               list(g_QT.cw()), g_QT.pi())
                        c.emit("vector", (lambda h, o=self.junkC[:], a=qrrot[:],
                                          b=cq_st: h.tensor_mul(o, a, b)),
                               g_QT.cw(), g_QT.pi())
                        c.emit("vector",
                               (lambda h, o=QT[:, oc * 512:(oc + 1) * 512],
                                a=self.junkB[:], b=self.junkC[:]:
                                h.tensor_add(o, a, b)), g_QT.cw(), g_QT.pi())
                        g_qrf.freed(g_QT)
                        if oc == 15:
                            g_rope.freed(g_QT)
                if n == 0:
                    self.emit_cc_kts(2 + oh)
            qtread = g_QT.cw()

            slots = [self.pg[0], self.pg[1], self.pg[2], self.pg[3],
                     self.pg[7]]
            svs = [self.pb[0], self.pb[1], self.pb[2], self.pb[3],
                   self.pb[7]]
            pOs = [self.pg[4], self.pg[5]]
            sg = [0]

            def slot():
                s = sg[0] % 5
                sg[0] += 1
                return slots[s], svs[s]

            h6w = list(self.pg[6].pw())
            nch_ready = {}
            junkB = self.junkB
            junkC_r = self.junkC

            def emit_nch_a(h_):
                hp = h_ % 2
                nrow = (junkB if hp == 0 else junkC_r)[0:1, 0:512]
                kb = 512 + 4 * hp
                sl, sv = slot()
                c.emit("scalar", (lambda h, o=junkQ[:],
                                  i=QT[:, h_ * 512:(h_ + 1) * 512]:
                                  h.activation(out=o, in_=i, func=AF.Square)),
                       list(g_QT.cw()) + list(g_nch.cw()), g_nch.pi())
                self.mm(sv[0:1, 0:512], self.ones_cb[:], junkQ[:],
                        True, True, list(sl.pw()) + list(g_nch.cw()),
                        sl.pi())
                c.emit("scalar", (lambda h, o=nrow,
                                  i=sv[0:1, 0:512]: h.copy(o, i)),
                       list(sl.cw()) + list(g_nch.cw()), g_nch.pi())
                sl.freed(g_nch)
                c.emit("vector", (lambda h, o=self.krow[0:1, kb:kb + 1],
                                  i=nrow:
                                  h.reduce_max(o, i, axis=AX.X)),
                       g_nch.cw(), g_nch.pi())
                kc0 = h_ * self.NKT
                c.emit("vector", (lambda h, o=self.krow[0:1, kb + 3:kb + 4],
                                  i=kmxB[0:1, kc0:kc0 + nkt]:
                                  h.reduce_max(o, i, axis=AX.X)),
                       list(g_nch.cw()) + list(pre_r), g_nch.pi())
                c.emit("vector", (lambda h, o=self.krow[0:1, kb + 1:kb + 2],
                                  i=self.krow[0:1, kb:kb + 1],
                                  s1=self.krow[0:1, kb + 3:kb + 4]:
                                  h.tensor_scalar(o, i, s1, 1.0,
                                                  op0=ALU.mult, op1=ALU.mult)),
                       g_nch.cw(), g_nch.pi())
                c.emit("scalar", (lambda h, o=self.krow[0:1, kb + 2:kb + 3],
                                  i=self.krow[0:1, kb + 1:kb + 2]:
                                  h.activation(out=o, in_=i, func=AF.Sqrt)),
                       g_nch.cw(), g_nch.pi())

            def emit_nch_b(h_):
                hp = h_ % 2
                kb = 512 + 4 * hp
                sl2, sv2 = slot()
                self.mm(sv2[:, 0:1], self.ones[:],
                        self.krow[0:1, kb + 2:kb + 3],
                        True, True, list(sl2.pw()) + list(g_nch.cw()),
                        sl2.pi())
                c.emit("scalar", (lambda h, o=nch_sb[:, h_:h_ + 1],
                                  i=sv2[:, 0:1]:
                                  h.activation(out=o, in_=i, func=AF.Copy,
                                               scale=-1.0)),
                       list(sl2.cw()) + list(g_nch.cw()), g_nch.pi())
                sl2.freed(g_nch)
                nch_ready[h_] = (g_nch.s, g_nch.s.n)

            def emit_fills(h_):
                hb = h_ % 2
                bw = list(bendw) if h_ < 2 else []
                bwv = list(bw)
                if n == 0:
                    bw = bw + list(self.ccw_g[h_ // 4])
                    if h_ < 2:
                        bwv = bwv + list(self.ccw_v)
                SH = self.SH
                n0 = min(nk, SH)
                ktg = self.kts2g[h_ // 4]
                hsub = h_ % 4
                c.dma("gpsimd", kh[hb][:, 0:n0], ktg[0, hsub, :, 0:n0],
                      list(g_kh[hb].pw()) + bw, g_kh[hb].pi(16))
                if nk > SH:
                    c.dma("gpsimd", kh[hb][:, SH:nk],
                          ktg[1, hsub, :, 0:nk - SH],
                          (), g_kh[hb].pi(16), first=False)
                c.dma("gpsimd",
                      vh[hb][:, 0:n0].rearrange("p (ck d) -> p ck d", d=128),
                      self.vrows2[0, 0:n0, h_ * 128:(h_ + 1) * 128].rearrange(
                          "(ck kp) d -> kp ck d", kp=128),
                      list(g_vh[hb].pw()) + bwv, g_vh[hb].pi(16))
                if nk > SH:
                    c.dma("gpsimd",
                          vh[hb][:, SH:nk].rearrange(
                              "p (ck d) -> p ck d", d=128),
                          self.vrows2[1, 0:nk - SH,
                                      h_ * 128:(h_ + 1) * 128].rearrange(
                              "(ck kp) d -> kp ck d", kp=128),
                          (), g_vh[hb].pi(16), first=False)

            pti = [0]

            def emit_st(h_, cc):
                hb = h_ % 2
                sl, sv = slot()
                w = list(sl.pw())
                if cc == 0:
                    w += list(g_kh[hb].cw())
                    if h_ == 0:
                        w += list(qtread)
                self.mm(sv[:, 0:512], kh[hb][:, cc * 128:(cc + 1) * 128],
                        QT[:, h_ * 512:(h_ + 1) * 512], True, True, w,
                        sl.pi())
                if cc == nkc - 1:
                    g_kh[hb].freed(sl)
                    if h_ == 15:
                        g_QT.freed(sl)
                p = pti[0] % 4
                pti[0] += 1
                pt = g_pt[p]
                pv = pts4[p]
                c.emit("scalar", (lambda h, o=pv, i=sv[:, 0:512],
                                  bb=nch_sb[:, h_:h_ + 1]:
                                  h.activation(out=o, in_=i, func=AF.Exp,
                                               bias=bb, scale=1.0)),
                       list(sl.cw()) + list(pt.pw()) + [nch_ready[h_]],
                       pt.pi())
                sl.freed(pt)
                if cc >= nkc - 8:
                    moff = (cc - (nkc - 8)) * 512
                    mw = list(pt.cw())
                    if h_ == 0 and n == 0 and cc == nkc - 8:
                        mw += list(pre_r)
                    c.emit("vector", (lambda h, o=pv, a=pv,
                                      m=mbt[:, moff:moff + 512]:
                                      h.tensor_mul(o, a, m)), mw, pt.pi())
                return p

            def emit_avl(h_, cc, p):
                hp = h_ % 2
                pO = pOs[hp]
                pt = g_pt[p]
                pv = pts4[p]
                w = list(pt.cw())
                if cc == 0:
                    w += list(pO.pw()) + list(g_vh[hp].cw())
                self.mm(self.pb[4 + hp][:, 0:512],
                        vh[hp][:, cc * 128:(cc + 1) * 128], pv,
                        cc == 0, cc == nkc - 1, w,
                        pO.pi() if cc == nkc - 1 else g_avp.pi())
                lw = list(pt.cw())
                pp = 32 * hp
                if cc == 0:
                    lw += list(g_l6[hp].pw())
                    if h_ < 2:
                        lw += h6w
                self.mm(self.pb[6][pp:pp + 1, 0:512], self.ones_cb[:], pv,
                        cc == 0, cc == nkc - 1, lw,
                        g_l6[hp].pi() if cc == nkc - 1 else g_avp.pi())
                pt.freed(g_l6[hp] if cc == nkc - 1 else g_avp)
                if cc == nkc - 1:
                    g_vh[hp].freed(pO)

            def emit_rl_a(h_):
                hp = h_ % 2
                pp = 32 * hp
                c.emit("scalar", (lambda h, o=lrow_sb[0:1, 0:512],
                                  i=self.pb[6][pp:pp + 1, 0:512]: h.copy(o, i)),
                       list(g_l6[hp].cw()) + list(g_rls.pw()), g_rls.pi())
                g_l6[hp].freed(g_rls)
                self.pg[6].freed(g_rls)
                c.emit("scalar", (lambda h, o=lrow_sb[0:1, 512:1024], i=lrow_sb[0:1, 0:512]:
                                  h.activation(out=o, in_=i, func=AF.Ln)),
                       g_rls.cw(), g_rls.pi())
                c.emit("scalar", (lambda h, o=lrow_sb[0:1, 0:512],
                                  i=lrow_sb[0:1, 512:1024]:
                                  h.activation(out=o, in_=i, func=AF.Exp,
                                               scale=-1.0)),
                       g_rls.cw(), g_rls.pi())

            def emit_rl_b(h_):
                hp = h_ % 2
                pO = pOs[hp]
                sl, sv = slot()
                self.mm(sv[:, 0:512], self.ones[:],
                        lrow_sb[0:1, 0:512], True, True,
                        list(sl.pw()) + list(g_rls.cw()), sl.pi())
                g_rls.freed(sl)
                c.emit("scalar", (lambda h, o=rlbc[:], i=sv[:, 0:512]:
                                  h.copy(o, i)),
                       list(sl.cw()) + list(g_rlb.pw()), g_rlb.pi())
                sl.freed(g_rlb)
                c.emit("vector", (lambda h, o=aoT[:, h_ * 512:(h_ + 1) * 512],
                                  a=self.pb[4 + hp][:, 0:512], b=rlbc[:]:
                                  h.tensor_mul(o, a, b)),
                       list(pO.cw()) + list(g_rlb.cw()) +
                       (list(g_aoT.pw()) if h_ == 0 else []),
                       g_aoT.pi())
                pO.freed(g_aoT)
                g_rlb.freed(g_aoT)

            LAG = 4
            emit_nch_a(0)
            if NH > 1:
                emit_nch_a(1)
            emit_nch_b(0)
            if NH > 1:
                emit_nch_b(1)
            emit_fills(0)
            q = []
            pend_rl = None
            for h_ in range(NH):
                for cc in range(nkc):
                    if len(q) >= LAG:
                        ph, pc, pp_ = q.pop(0)
                        emit_avl(ph, pc, pp_)
                        if pc == nkc - 1:
                            pend_rl = ph
                    q.append((h_, cc, emit_st(h_, cc)))
                    if cc == 1 and h_ + 2 < NH:
                        emit_nch_a(h_ + 2)
                    if cc == 3 and h_ + 1 < NH:
                        emit_fills(h_ + 1)
                    if cc == 4 and pend_rl is not None:
                        emit_rl_a(pend_rl)
                    if cc == 6 and h_ + 2 < NH:
                        emit_nch_b(h_ + 2)
                    if cc == 7 and pend_rl is not None:
                        emit_rl_b(pend_rl)
                        pend_rl = None
            while q:
                ph, pc, pp_ = q.pop(0)
                emit_avl(ph, pc, pp_)
            if pend_rl is not None:
                emit_rl_a(pend_rl)
                emit_rl_b(pend_rl)
            emit_rl_a(NH - 1)
            emit_rl_b(NH - 1)
            aor = g_aoT.cw()
            prev_aor = aor
            for jh in range(2):
                for ic in range(16):
                    i = ic % 2
                    c.dma("sync", wotr[i][:],
                          self.wots[ic * 128:(ic + 1) * 128,
                                    jh * 1024:(jh + 1) * 1024],
                          g_wot[i].pw(), g_wot[i].pi(16))
                    for s_ in range(4):
                        for si in range(2):
                            bk = self.pg[2 * s_ + si]
                            w = []
                            if ic == 0:
                                w = list(bk.pw())
                                if s_ == 0 and si == 0:
                                    w += list(aor)
                            if si == 0:
                                w = w + list(g_wot[i].cw())
                            last = s_ == 3 and si == 1
                            inc = bk.pi() if ic == 15 else (g_icp.pi() if last else ())
                            self.mm(self.pb[2 * s_ + si][:, 0:512],
                                    aoT[:, ic * 512 + s_ * 128:
                                        ic * 512 + (s_ + 1) * 128],
                                    wotr[i][:, si * 512:(si + 1) * 512],
                                    ic == 0, ic == 15, w, inc)
                            if last:
                                g_wot[i].freed(g_icp if ic < 15 else bk)
                                if jh == 1 and ic == 15:
                                    g_aoT.freed(bk)
                for s_ in range(4):
                    for si in range(2):
                        bk = self.pg[2 * s_ + si]
                        dst = x1pre[:, s_ * D + jh * 1024 + si * 512:
                                    s_ * D + jh * 1024 + (si + 1) * 512]
                        c.emit("scalar", (lambda h, o=dst,
                                          i=self.pb[2 * s_ + si][:, 0:512]:
                                          h.copy(o, i)),
                               list(bk.cw()) +
                               (list(g_x1p.pw())
                                if (jh == 0 and s_ == 0 and si == 0) else []),
                               g_x1p.pi())
                        bk.freed(g_x1p)
            for s_ in range(4):
                i = s_ % 2
                c.dma("sync", xqr[i],
                      self.xq[n * 512 + s_ * 128: n * 512 + (s_ + 1) * 128, :],
                      list(g_xqr[i].pw()) + list(aor), g_xqr[i].pi(16))
                c.emit("vector", (lambda h, o=x1pre[:, s_ * D:(s_ + 1) * D],
                                  a=x1pre[:, s_ * D:(s_ + 1) * D], b=xqr[i]:
                                  h.tensor_add(o, a, b)),
                       list(g_xqr[i].cw()) + list(g_x1p.cw()), g_x1p.pi())
                g_xqr[i].freed(g_x1p)
                x1v = x1pre[:, s_ * D:(s_ + 1) * D]
                c.emit("vector", None, g_x1f.pw(), ())
                self.ln_chain(x1v, D, x1v,
                              g_lvs, g_lsv, g_x1f, pre_gate=g_x1p,
                              pre_ready=g_x1p.cw())
                c.dma("sync",
                      self.x1s[n * 512 + s_ * 128: n * 512 + (s_ + 1) * 128, :],
                      x1v, g_x1f.cw(), g_x1d.pi(16))
                for db in range(4):
                    si = db % 2
                    for k in range(4):
                        hc = db * 4 + k
                        w = []
                        if k == 0:
                            w = list(ptrb.pw()) + list(g_x1f.cw())
                        self.tr(self.pb[7][:, k * 128:(k + 1) * 128],
                                x1v[:, hc * 128:(hc + 1) * 128], self.iff[:],
                                w, ptrb.pi() if k == 3 else ())
                    c.emit("scalar", (lambda h, o=x1Tst[si][:],
                                      i=self.pb[7][:, 0:512]:
                                      h.copy(o, i)),
                           list(ptrb.cw()) + list(g_x1T[si].pw()),
                           g_x1T[si].pi())
                    ptrb.freed(g_x1T[si])
                    if db == 3:
                        g_x1f.freed(g_x1T[si])
                    col = n * 512 + s_ * 128
                    for k in range(4):
                        hc = db * 4 + k
                        c.dma("sync", self.x1ts[hc * 128:(hc + 1) * 128,
                                                col:col + 128],
                              x1Tst[si][:, k * 128:(k + 1) * 128],
                              g_x1T[si].cw() if k == 0 else (),
                              g_x1d.pi(16), first=k == 0)
                    g_x1T[si].freed(g_x1d)
            bendw.clear()
            bendw.extend(list(g_x1p.cw()) + list(g_x1f.cw()) +
                         list(g_x1T[0].cw()) + list(g_x1T[1].cw()) +
                         list(g_x1d.cw()))

    def phase_c(self):
        c = self.ctx
        nc = self.nc
        TP, CP = self.TP, self.CP
        NTS = TP // 128
        NTT = TP // 512
        sb = lambda n, sh, ty: self.phstack.enter_context(nc.sbuf_tensor(n, sh, ty))
        x1T = sb("x1T", [128, 16 * TP], BF16)
        acc = sb("acc", [128, NTS * D], F32)
        gsl = sb("gsl", [128, 16 * 512], BF16)
        usl = sb("usl", [128, 16 * 512], BF16)
        dsl = sb("dsl", [128, 4 * D], BF16)
        fT = sb("fT", [128, 4 * TP], BF16)
        gT = sb("gT", [128, 512], BF16)
        uT = sb("uT", [128, 512], BF16)
        sdb_sb = sb("sdb_sb", [128, D], F32)
        xf = sb("xf0", [128, D], F32)
        pre2s = [sb(f"pre2{i}", [128, D], F32) for i in range(2)]

        g_x1T = Gate(c, "c.x1T")
        g_gs = Gate(c, "c.gs", kind="g")
        g_us = Gate(c, "c.us", kind="g")
        g_ds = Gate(c, "c.ds", kind="g")
        g_gT = Gate(c, "c.gT")
        g_uT = Gate(c, "c.uT")
        g_fT = Gate(c, "c.fT")
        g_acc = Gate(c, "c.acc")
        g_xf = Gate(c, "c.xf")
        g_outs = [Gate(c, f"c.out{i}") for i in range(2)]
        g_lvss = [Gate(c, f"c.lvs{i}") for i in range(2)]
        g_lsvs = [Gate(c, f"c.lsv{i}") for i in range(2)]
        g_pre2s = [Gate(c, f"c.pre2{i}") for i in range(2)]
        self.out_gate = Gate(c, "c.outd")
        g_sd = Gate(c, "c.sd")
        c.dma("sync", sdb_sb[:], self.sdb, (), g_sd.pi(16))

        NOB = INTER // 512
        finw = []

        def load_x1T(cp_):
            tb = cp_ * TP
            for hc in range(16):
                c.dma("sync", x1T[:, hc * TP:(hc + 1) * TP],
                      self.x1ts[hc * 128:(hc + 1) * 128, tb:tb + TP],
                      g_x1T.pw() if hc == 0 else (), g_x1T.pi(16),
                      first=hc == 0)
            return g_x1T.cw()

        x1r = load_x1T(0)
        for cp in range(CP):
            t0 = cp * TP
            for ob in range(NOB):
                for hc in range(16):
                    c.dma("gpsimd", gsl[:, hc * 512:(hc + 1) * 512],
                          self.wg8[hc * 128:(hc + 1) * 128,
                                   ob * 512:(ob + 1) * 512],
                          g_gs.pw() if hc == 0 else (), g_gs.pi(16),
                          first=hc == 0)
                    c.dma("gpsimd", usl[:, hc * 512:(hc + 1) * 512],
                          self.wu8[hc * 128:(hc + 1) * 128,
                                   ob * 512:(ob + 1) * 512],
                          g_us.pw() if hc == 0 else (), g_us.pi(16),
                          first=hc == 0)
                for ii in range(4):
                    c.dma("gpsimd", dsl[:, ii * D:(ii + 1) * D],
                          self.wd8[ob * 512 + ii * 128: ob * 512 + (ii + 1) * 128, :],
                          g_ds.pw() if ii == 0 else (), g_ds.pi(16),
                          first=ii == 0)
                for tt in range(NTT):
                    for ocl in range(4):
                        oc = ob * 4 + ocl
                        bk = self.pg[ocl]
                        for hc in range(16):
                            w = []
                            if hc == 0:
                                w = list(bk.pw()) + list(g_gs.cw())
                                if tt == 0 and ocl == 0 and ob == 0:
                                    w += list(x1r)
                            self.mm(self.pb[ocl][:, 0:512],
                                    gsl[:, hc * 512 + ocl * 128:
                                        hc * 512 + (ocl + 1) * 128],
                                    x1T[:, hc * TP + tt * 512:
                                        hc * TP + (tt + 1) * 512],
                                    hc == 0, hc == 15, w,
                                    bk.pi() if hc == 15 else ())
                            if hc == 15 and tt == NTT - 1 and ocl == 3:
                                g_gs.freed(bk)
                        c.emit("scalar", (lambda h, o=gT[:],
                                          i=self.pb[ocl][:, 0:512],
                                          sc=self.sgt_sb[:, oc:oc + 1]:
                                          h.activation(out=o, in_=i,
                                                       func=AF.Silu,
                                                       scale=sc)),
                               list(bk.cw()) + list(g_gT.pw()), g_gT.pi())
                        bk.freed(g_gT)
                        bu = self.pg[4 + ocl]
                        for hc in range(16):
                            w = []
                            if hc == 0:
                                w = list(bu.pw()) + list(g_us.cw())
                            self.mm(self.pb[4 + ocl][:, 0:512],
                                    usl[:, hc * 512 + ocl * 128:
                                        hc * 512 + (ocl + 1) * 128],
                                    x1T[:, hc * TP + tt * 512:
                                        hc * TP + (tt + 1) * 512],
                                    hc == 0, hc == 15, w,
                                    bu.pi() if hc == 15 else ())
                            if hc == 15 and tt == NTT - 1 and ocl == 3:
                                g_us.freed(bu)
                                if ob == NOB - 1:
                                    g_x1T.freed(bu)
                        c.emit("scalar", (lambda h, o=uT[:],
                                          i=self.pb[4 + ocl][:, 0:512],
                                          sc=self.sut_sb[:, oc:oc + 1]:
                                          h.activation(out=o, in_=i,
                                                       func=AF.Copy,
                                                       scale=sc)),
                               list(bu.cw()) + list(g_uT.pw()), g_uT.pi())
                        bu.freed(g_uT)
                        c.emit("vector", (lambda h,
                                          o=fT[:, ocl * TP + tt * 512:
                                               ocl * TP + (tt + 1) * 512],
                                          a=gT[:], b=uT[:]:
                                          h.tensor_mul(o, a, b)),
                               list(g_gT.cw()) + list(g_uT.cw()) +
                               (list(g_fT.pw()) if (tt == 0 and ocl == 0) else []),
                               g_fT.pi())
                        g_gT.freed(g_fT)
                        g_uT.freed(g_fT)
                ftr = g_fT.cw()
                for ts in range(NTS):
                    b0 = 4 * (ts % 2)
                    bks = self.pg[b0:b0 + 4]
                    for ii in range(4):
                        for si in range(4):
                            bk = bks[si]
                            w = []
                            if ii == 0:
                                w = list(bk.pw())
                                if si == 0:
                                    w += list(ftr) + list(g_ds.cw())
                            self.mm(self.pb[b0 + si][:, 0:512],
                                    fT[:, ii * TP + ts * 128:
                                       ii * TP + (ts + 1) * 128],
                                    dsl[:, ii * D + si * 512:
                                        ii * D + (si + 1) * 512],
                                    ii == 0, ii == 3, w,
                                    bk.pi() if ii == 3 else ())
                            if ii == 3 and si == 3 and ts == NTS - 1:
                                g_ds.freed(bk)
                                g_fT.freed(bk)
                    for si in range(4):
                        bk = bks[si]
                        dst = acc[:, ts * D + si * 512: ts * D + (si + 1) * 512]
                        if ob == 0:
                            c.emit("scalar",
                                   (lambda h, o=dst,
                                    i=self.pb[b0 + si][:, 0:512]:
                                    h.copy(o, i)),
                                   list(bk.cw()) +
                                   (list(g_acc.pw()) + list(finw)
                                    if (ts == 0 and si == 0) else []),
                                   g_acc.pi())
                        else:
                            c.emit("vector",
                                   (lambda h, o=dst,
                                    i=self.pb[b0 + si][:, 0:512]:
                                    h.tensor_add(o, i, o)),
                                   list(bk.cw()) +
                                   (list(g_acc.cw())
                                    if (ob == 1 and ts == 0 and si == 0)
                                    else []),
                                   g_acc.pi())
                        bk.freed(g_acc)
            if cp + 1 < CP:
                x1r = load_x1T(cp + 1)
            accr = g_acc.cw()
            for ts in range(NTS):
                i = ts % 2
                row = t0 + ts * 128
                c.dma("sync", xf[:], self.x1s[row:row + 128, :],
                      g_xf.pw(), g_xf.pi(16))
                c.emit("vector", (lambda h, o=pre2s[i][:],
                                  a=acc[:, ts * D:(ts + 1) * D], b=sdb_sb[:]:
                                  h.tensor_mul(o, a, b)),
                       list(accr) + list(g_sd.cw()) + list(g_pre2s[i].pw()),
                       g_pre2s[i].pi())
                c.emit("vector", (lambda h, o=pre2s[i][:], a=pre2s[i][:],
                                  b=xf[:]: h.tensor_add(o, a, b)),
                       list(g_xf.cw()) + list(g_pre2s[i].cw()),
                       g_pre2s[i].pi())
                g_xf.freed(g_pre2s[i])
                if ts == NTS - 1 and cp == CP - 1:
                    g_acc.freed(g_pre2s[i])
                self.ln_chain(pre2s[i][:], D, pre2s[i][:], g_lvss[i],
                              g_lsvs[i], g_outs[i], pre_gate=g_pre2s[i],
                              pre_ready=g_pre2s[i].cw(), stats_base=5 * i,
                              junk=acc[:, ts * D:(ts + 1) * D])
                c.dma("sync", self.out[row:row + 128, :], pre2s[i][:],
                      g_outs[i].cw(), self.out_gate.pi(16))
                g_outs[i].freed(self.out_gate)
                g_pre2s[i].freed(self.out_gate)
            finw = list(g_lsvs[0].cw()) + list(g_lsvs[1].cw())


_CACHE = {}


def _rope_tables(S):
    freqs = 1.0 / (10000.0 ** (np.arange(0, DH, 2, dtype=np.float64) / DH))
    ang = np.arange(S, dtype=np.float64)[:, None] * freqs[None, :]
    cos = np.concatenate([np.cos(ang), np.cos(ang)], -1).astype(np.float32)
    sin = np.concatenate([np.sin(ang), np.sin(ang)], -1).astype(np.float32)
    return cos, sin


def _q_rows(p, S):
    rows = []
    for m in range(S // 128):
        if m % 2 == p:
            rows.extend(range(128 * m, 128 * m + 128))
    return np.array(rows)


def _masks(p):
    out = np.zeros((128, 4096), dtype=NPBF)
    r = np.arange(128)[:, None]
    qq = np.arange(512)[None, :]
    for cc in range(8):
        vis = (128 * cc + r) <= (256 * (qq // 128) + 128 * p + (qq % 128))
        out[:, cc * 512:(cc + 1) * 512] = vis.astype(NPBF)
    return out


def kernel(L0_attn_x, L0_attn_W_dq, L0_attn_W_uq, L0_attn_W_qr, L0_attn_W_dkv,
           L0_attn_W_ukv, L0_attn_W_kr, L0_attn_W_o, L0_attn_q_ln_w,
           L0_attn_q_ln_b, L0_attn_kv_ln_w, L0_attn_kv_ln_b, L0_ffn_w_gate,
           L0_ffn_w_up, L0_ffn_w_down, L0_ffn_s_gate, L0_ffn_s_up,
           L0_ffn_s_down):
    S = 4096
    x = np.ascontiguousarray(np.asarray(L0_attn_x, dtype=np.float32))
    f32 = lambda a: np.ascontiguousarray(np.asarray(a, dtype=np.float32))
    wdq, wuq, wqr = f32(L0_attn_W_dq), f32(L0_attn_W_uq), f32(L0_attn_W_qr)
    wdkv, wukv, wkr = f32(L0_attn_W_dkv), f32(L0_attn_W_ukv), f32(L0_attn_W_kr)
    wo = f32(L0_attn_W_o)
    wqcat = np.ascontiguousarray(np.concatenate([wuq, wqr], axis=1))
    wg8 = np.ascontiguousarray(np.asarray(L0_ffn_w_gate, dtype=np.int8))
    wu8 = np.ascontiguousarray(np.asarray(L0_ffn_w_up, dtype=np.int8))
    wd8 = np.ascontiguousarray(np.asarray(L0_ffn_w_down, dtype=np.int8))
    sg = f32(L0_ffn_s_gate).reshape(64, 128).T.copy()
    su = f32(L0_ffn_s_up).reshape(64, 128).T.copy()
    sdb = np.ascontiguousarray(np.broadcast_to(f32(L0_ffn_s_down), (128, D)))
    qlw = np.ascontiguousarray(np.broadcast_to(f32(L0_attn_q_ln_w), (128, QP)))
    qlb = np.ascontiguousarray(np.broadcast_to(f32(L0_attn_q_ln_b), (128, QP)))
    kvlw = np.ascontiguousarray(np.broadcast_to(f32(L0_attn_kv_ln_w), (128, KV)))
    kvlb = np.ascontiguousarray(np.broadcast_to(f32(L0_attn_kv_ln_b), (128, KV)))
    cos, sin = _rope_tables(S)
    coskT = np.ascontiguousarray(cos.T)
    sinkT = np.ascontiguousarray(sin.T)
    identb = np.eye(128, dtype=NPBF)
    identf = np.eye(128, dtype=np.float32)

    if S not in _CACHE:
        _CACHE[S] = Builder(S).build()
    nc = _CACHE[S]

    in_maps = []
    rowmaps = []
    SH = S // 2
    for c in range(8):
        b, p = c >> 1, c & 1
        rows = _q_rows(p, S)
        rowmaps.append((b, rows))
        ks = slice(p * SH, (p + 1) * SH)
        in_maps.append({
            "xkv": np.ascontiguousarray(x[b][ks]),
            "xq": np.ascontiguousarray(x[b][rows]),
            "wdq": wdq, "wqcat": wqcat, "wdkv": wdkv, "wukv": wukv,
            "wkr": wkr, "wo": wo, "wg8": wg8, "wu8": wu8, "wd8": wd8,
            "sgt": sg, "sut": su, "sdb": sdb, "qlw": qlw, "qlb": qlb,
            "kvlw": kvlw, "kvlb": kvlb,
            "coskT": np.ascontiguousarray(coskT[:, ks]),
            "sinkT": np.ascontiguousarray(sinkT[:, ks]),
            "cosqT": np.ascontiguousarray(cos[rows].T),
            "sinqT": np.ascontiguousarray(sin[rows].T),
            "maskst": _masks(p), "identb": identb, "identf": identf,
        })
    res = run_bass_kernel_spmd(nc, in_maps, core_ids=list(range(8)))
    out = np.empty_like(x)
    for c in range(8):
        b, rows = rowmaps[c]
        out[b][rows] = res.results[c]["out"]
    return out





# version 25 (final)
# speedup vs baseline: 1.0307x; 1.0307x over previous
import numpy as np
import ml_dtypes
import concourse.bass as bass
import concourse.mybir as mybir
from concourse.bass_utils import run_bass_kernel_spmd

F32 = mybir.dt.float32
BF16 = mybir.dt.bfloat16
I8 = mybir.dt.int8
U8 = mybir.dt.uint8
AF = mybir.ActivationFunctionType
ALU = mybir.AluOpType
AX = mybir.AxisListType
NPBF = ml_dtypes.bfloat16

D = 2048
QP = 1024
KV = 1365
KVC = 11
QD = 2048
NH = 16
DH = 128
INTER = 8192
NEG = -1e30
EPS = 1e-5
SCALE = 1.0 / float(np.sqrt(np.float64(DH)))
ENGINES = ("tensor", "scalar", "vector", "sync", "gpsimd")


def kvrows(c):
    return 128 if c < KVC - 1 else KV - 128 * (KVC - 1)


class Sem:
    __slots__ = ("h", "n", "name", "kind")

    def __init__(self, h, name):
        self.h, self.n, self.name = h, 0, name
        self.kind = None


class Gate:

    def __init__(self, ctx, name, kind="o"):
        self.s = ctx.sem(name, kind)
        self.free = []

    def pi(self, amt=1):
        return ((self.s, amt),)

    def pw(self):
        w = tuple(self.free)
        self.free = []
        return w

    def cw(self):
        return ((self.s, self.s.n),)

    def freed(self, other):
        self.free.append((other.s, other.s.n))


class Ctx:
    def __init__(self, nc, stack):
        self.nc, self.stack = nc, stack
        self.streams = {e: [] for e in ENGINES}
        self._si = 0
        self.dma_open = {"sync": {}, "gpsimd": {}}
        self.n_inst = 0
        self.sem_pool = []
        self.scopes = []
        self.eng_incs = {e: {} for e in ENGINES}

    def sem(self, name=None, kind="o"):
        s = None
        for idx, cand in enumerate(self.sem_pool):
            if cand.kind is None or cand.kind == kind:
                s = self.sem_pool.pop(idx)
                s.name = name or s.name
                break
        if s is None:
            self._si += 1
            h = self.stack.enter_context(self.nc.semaphore(f"s{self._si}"))
            s = Sem(h, name or f"s{self._si}")
        s.kind = s.kind or kind
        if self.scopes:
            self.scopes[-1].append(s)
        return s

    def push_scope(self):
        self.scopes.append([])

    def pop_scope(self):
        self.sem_pool.extend(self.scopes.pop())

    def emit(self, eng, fn, waits=(), incs=(), track=True):
        assert len(incs) <= 1
        for s, amt in incs:
            assert s.kind == ("g" if (eng == "gpsimd" and amt == 16) else s.kind)
            s.n += amt
            if track:
                self.eng_incs[eng][id(s)] = (s, s.n)
        self.streams[eng].append((fn, tuple(waits), tuple(incs)))
        self.n_inst += 1

    def dma(self, eng, out, in_, waits=(), incs=(), first=True, slow=False):
        assert len(incs) == 1 and incs[0][1] == 16, "dma: exactly one +16 sem"
        sem = incs[0][0]
        want = "g" if eng == "gpsimd" else "o"
        assert sem.kind == want, f"sem {sem.name} kind {sem.kind} != {want}"
        if first:
            waits = ((sem, sem.n),) + tuple(waits)
        self.dma_open[eng][id(sem)] = sem
        if slow:
            self.emit(eng, (lambda h, o=out, i=in_: h.dma_start(
                out=o, in_=i, allow_slow_non_contiguous=True)), waits, incs)
        else:
            self.emit(eng, (lambda h, o=out, i=in_: h.dma_start(out=o, in_=i)),
                      waits, incs)

    def barrier(self):
        b = self.sem("bar")
        for e in ENGINES:
            waits = [(s, n) for (s, n) in self.eng_incs[e].values()]
            self.eng_incs[e] = {}
            if e in self.dma_open:
                waits += [(s, s.n) for s in self.dma_open[e].values()]
                self.dma_open[e] = {}
            self.emit(e, (lambda h: h.nop()), waits, ((b, 1),))
        tgt = b.n
        for e in ENGINES:
            self.emit(e, None, ((b, tgt),), ())

    def replay(self, block):
        for ename in ENGINES:
            ops = self.streams[ename]
            if not ops:
                continue

            def make(ops):
                def body(h):
                    maxw = {}
                    for fn, waits, incs in ops:
                        for s, v in waits:
                            if maxw.get(id(s), -1) >= v:
                                continue
                            h.wait_ge(s.h, v)
                            maxw[id(s)] = v
                        if fn is None:
                            for s, amt in incs:
                                h.nop().then_inc(s.h, amt)
                            continue
                        ins = fn(h)
                        for s, amt in incs:
                            ins.then_inc(s.h, amt)
                return body
            getattr(block, ename)(make(ops))


class Builder:
    def __init__(self, S=4096):
        self.S = S
        self.NKT = S // 512
        self.NKTA = self.NKT // 2
        self.SH = S // 2
        self.NQ = S // 2
        self.NQB = self.NQ // 512
        self.CP = max(1, self.NQ // 1024)
        self.TP = self.NQ // self.CP
        assert self.TP % 512 == 0

    def nkt_of(self, n):
        return 2 * n + 2

    def build(self):
        import contextlib
        nc = bass.Bass("TRN2", target_bir_lowering=False, debug=False)
        self.nc = nc
        S, NQ = self.S, self.NQ

        dt = lambda n, sh, ty, k="ExternalInput": nc.dram_tensor(n, sh, ty, kind=k).ap()
        self.xkv = dt("xkv", [self.SH, D], F32)
        self.xq = dt("xq", [NQ, D], F32)
        self.wdq = dt("wdq", [D, QP], F32)
        self.wqcat = dt("wqcat", [QP, QD], F32)
        self.wdkv = dt("wdkv", [D, KV], F32)
        self.wukv = dt("wukv", [KV, 3840], F32)
        self.wkr = dt("wkr", [D, 256], F32)
        self.wo = dt("wo", [D, D], F32)
        self.wg8 = dt("wg8", [D, INTER], I8)
        self.wu8 = dt("wu8", [D, INTER], I8)
        self.wd8 = dt("wd8", [INTER, D], I8)
        self.sgt = dt("sgt", [128, 64], F32)
        self.sut = dt("sut", [128, 64], F32)
        self.sdb = dt("sdb", [128, D], F32)
        self.qlw = dt("qlw", [128, QP], F32)
        self.qlb = dt("qlb", [128, QP], F32)
        self.kvlw = dt("kvlw", [128, KV], F32)
        self.kvlb = dt("kvlb", [128, KV], F32)
        self.coskT = dt("coskT", [128, self.SH], F32)
        self.sinkT = dt("sinkT", [128, self.SH], F32)
        self.cosqT = dt("cosqT", [128, NQ], F32)
        self.sinqT = dt("sinqT", [128, NQ], F32)
        self.maskst = dt("maskst", [128, 4096], BF16)
        self.identb = dt("identb", [128, 128], BF16)
        self.identf = dt("identf", [128, 128], F32)
        self.ktsh = dt("ktsh", [NH, 128, self.SH], BF16, "Internal")
        self.kts2g = [dt(f"kts2g{i}", [2, 4, 128, self.SH], BF16, "Internal")
                      for i in range(4)]
        self.vrowsh = dt("vrowsh", [self.SH, D], BF16, "Internal")
        self.vrows2 = dt("vrows2", [2, self.SH, D], BF16, "Internal")
        self.wots = dt("wots", [D, D], BF16, "Internal")
        self.ckvts = dt("ckvts", [KV, self.SH], BF16, "Internal")
        self.x1s = dt("x1s", [NQ, D], F32, "Internal")
        self.x1ts = dt("x1ts", [D, NQ], BF16, "Internal")
        self.kmsh = dt("kmsh", [1, NH * self.NKTA], F32, "Internal")
        self.kms2 = dt("kms2", [2, NH * self.NKTA], F32, "Internal")
        self.out = dt("out", [NQ, D], F32, "ExternalOutput")

        with contextlib.ExitStack() as stack:
            ctx = Ctx(nc, stack)
            self.ctx = ctx
            self.stack = stack
            sb = lambda n, sh, ty: stack.enter_context(nc.sbuf_tensor(n, sh, ty))
            self.ib = sb("ib", [128, 128], BF16)
            self.iff = sb("iff", [128, 128], F32)
            self.ones = sb("ones", [1, 128], F32)
            self.ones_cb = sb("ones_cb", [128, 1], BF16)
            self.krow = sb("krow", [1, 520], F32)
            self.kmxp = sb("kmxp", [1, NH * self.NKTA], F32)
            self.stats = sb("stats", [128, 16], F32)
            self.lcol = sb("lcol", [128, 4], F32)
            self.rlc = sb("rlc", [128, 4], F32)
            self.rls = sb("rls", [1, 512], F32)
            self.junkA = sb("junkA", [128, D], F32)
            self.junkB = sb("junkB", [128, 512], F32)
            self.junkC = sb("junkC", [128, 512], F32)
            self.sgt_sb = sb("sgt_sb", [128, 64], F32)
            self.sut_sb = sb("sut_sb", [128, 64], F32)
            self.eps_sb = sb("eps_sb", [128, 1], F32)

            self.pq = [stack.enter_context(nc.psum_tensor(f"pq{i}", [128, 1024], F32))
                       for i in range(4)]
            self.pb = [self.pq[i // 2][:][:, (i % 2) * 512:(i % 2) * 512 + 512]
                       for i in range(8)]
            self.pg = [Gate(ctx, f"pb{i}") for i in range(8)]

            g_pre = Gate(ctx, "pre")
            for fi, (out_sb, srcd) in enumerate(
                    ((self.ib, self.identb), (self.iff, self.identf),
                     (self.sgt_sb, self.sgt), (self.sut_sb, self.sut))):
                ctx.dma("sync", out_sb[:], srcd, (), g_pre.pi(16), first=fi == 0)
            g_init = Gate(ctx, "init")
            ctx.emit("vector", (lambda h: h.memset(self.ones[:], 1.0)), (),
                     g_init.pi())
            ctx.emit("vector", (lambda h: h.memset(self.ones_cb[:], 1.0)), (),
                     g_init.pi())
            ctx.emit("vector", (lambda h: h.memset(self.eps_sb[:], EPS)), (),
                     g_init.pi())
            ctx.emit("tensor", None, list(g_pre.cw()) + list(g_init.cw()), ())
            ctx.emit("scalar", None, list(g_pre.cw()) + list(g_init.cw()), ())
            ctx.emit("vector", None, list(g_pre.cw()), ())

            self.cc_gate = Gate(ctx, "ccd")
            for ph in (self.phase_wot, self.phase_a1, self.phase_a2,
                       self.phase_b, self.phase_c):
                ctx.push_scope()
                with contextlib.ExitStack() as pst:
                    self.phstack = pst
                    ph()
                ctx.barrier()
                ctx.pop_scope()

            ctx.emit("sync", None,
                     ((self.out_gate.s, self.out_gate.s.n),), ())

            with nc.Block() as block:
                ctx.replay(block)
        return nc

    def mm(self, pview, lhsT, rhs, start, stop, waits=(), incs=()):
        self.ctx.emit("tensor",
                      (lambda h, o=pview, a=lhsT, b=rhs, s1=start, s2=stop:
                       h.matmul(o, a, b, start=s1, stop=s2)), waits, incs)

    def tr(self, pview, inp, ident, waits=(), incs=()):
        self.ctx.emit("tensor",
                      (lambda h, o=pview, i=inp, d=ident: h.transpose(o, i, d)),
                      waits, incs)

    def ln_chain(self, pre_view, dim, out_view, gvs, gsv, gdone,
                 gamma=None, beta=None, out_tmp=None, pre_gate=None,
                 pre_ready=(), stats_base=0, junk=None):
        c = self.ctx
        st = self.stats
        if junk is None:
            junk = self.junkA
        msum, negm, sumsq, sdv, istd = (
            st[:, stats_base + i:stats_base + i + 1] for i in range(5))
        c.emit("vector", (lambda h, o=msum, i=pre_view:
                          h.reduce_sum(o, i, axis=AX.X)),
               tuple(pre_ready) + gvs.cw() + gdone.cw(), gvs.pi())
        c.emit("vector", (lambda h, o=negm, i=msum, k=-1.0 / dim:
                          h.tensor_scalar_mul(o, i, k)), gvs.cw(), gvs.pi())
        c.emit("scalar", (lambda h, o=junk[:, :dim], i=pre_view, b=negm,
                          a=sumsq:
                          h.activation(out=o, in_=i, func=AF.Square, bias=b,
                                       scale=1.0, accum_out=a)),
               gvs.cw() + gsv.cw(), gsv.pi())
        c.emit("scalar", (lambda h, o=sdv, i=sumsq, k=1.0 / dim:
                          h.activation(out=o, in_=i, func=AF.Sqrt,
                                       bias=self.eps_sb[:], scale=k)),
               gsv.cw(), gsv.pi())
        c.emit("vector", (lambda h, o=istd, i=sdv: h.reciprocal(o, i)),
               gsv.cw() + gdone.cw(), gdone.pi())
        if gamma is None:
            c.emit("vector", (lambda h, o=out_view, i=pre_view, a=negm, b=istd:
                              h.tensor_scalar(o, i, a, b, op0=ALU.add,
                                              op1=ALU.mult)),
                   gdone.cw(), gdone.pi())
        else:
            z = out_tmp
            c.emit("vector", (lambda h, o=z, i=pre_view, a=negm, b=istd:
                              h.tensor_scalar(o, i, a, b, op0=ALU.add,
                                              op1=ALU.mult)),
                   gdone.cw(), gdone.pi())
            c.emit("vector", (lambda h, o=z, i=z, g=gamma:
                              h.tensor_mul(o, i, g)), gdone.cw(), gdone.pi())
            c.emit("vector", (lambda h, o=out_view, i=z, g=beta:
                              h.tensor_add(o, i, g)), gdone.cw(), gdone.pi())
        if pre_gate is not None:
            pre_gate.freed(gdone)

    def knorm_chain(self, src_view, src_wait, src_freed_gate, h_, kt,
                    ksq, g_kn, pbank_gate, pbank_view):
        c = self.ctx
        NKT = self.NKT
        c.emit("scalar", (lambda h, o=ksq[:], i=src_view:
                          h.activation(out=o, in_=i, func=AF.Square)),
               tuple(src_wait) + g_kn.cw(), g_kn.pi())
        if src_freed_gate is not None:
            src_freed_gate.freed(g_kn)
        self.mm(pbank_view[0:1, 0:512], self.ones_cb[:], ksq[:],
                True, True, list(pbank_gate.pw()) + list(g_kn.cw()),
                pbank_gate.pi())
        c.emit("scalar", (lambda h, o=self.krow[0:1, 0:512],
                          i=pbank_view[0:1, 0:512]: h.copy(o, i)),
               list(pbank_gate.cw()) + list(g_kn.cw()), g_kn.pi())
        pbank_gate.freed(g_kn)
        c.emit("vector", (lambda h, o=self.krow[0:1, 512:513],
                          i=self.krow[0:1, 0:512]:
                          h.reduce_max(o, i, axis=AX.X)),
               g_kn.cw(), g_kn.pi())
        col = h_ * NKT + kt
        if kt == 0:
            c.emit("vector", (lambda h, o=self.kmxp[0:1, col:col + 1],
                              i=self.krow[0:1, 512:513]:
                              h.tensor_copy(o, i)), g_kn.cw(), g_kn.pi())
        else:
            c.emit("vector", (lambda h, o=self.kmxp[0:1, col:col + 1],
                              a=self.kmxp[0:1, col - 1:col],
                              b=self.krow[0:1, 512:513]:
                              h.tensor_max(o, a, b)), g_kn.cw(), g_kn.pi())

    def phase_wot(self):
        c = self.ctx
        nc = self.nc
        sb = lambda n, sh, ty: self.phstack.enter_context(nc.sbuf_tensor(n, sh, ty))
        wob = [sb(f"wob{i}", [128, D], BF16) for i in range(4)]
        wost = [sb(f"wost{i}", [128, 512], BF16) for i in range(2)]
        g_ld = [Gate(c, f"wob{i}", kind="g") for i in range(4)]
        g_st = [Gate(c, f"wost{i}") for i in range(2)]
        g_wd = Gate(c, "wotd")
        bank = self.pg[0]
        bt = self.pb[0].bitcast(BF16)
        for rcb in range(4):
            for k in range(4):
                rc = rcb * 4 + k
                c.dma("gpsimd", wob[k][:], self.wo[rc * 128:(rc + 1) * 128, :],
                      g_ld[k].pw(), g_ld[k].pi(16))
            for jc in range(16):
                si = (rcb * 16 + jc) % 2
                for k in range(4):
                    w = []
                    if k == 0:
                        w = list(bank.pw())
                        if jc == 0:
                            for kk in range(4):
                                w += list(g_ld[kk].cw())
                    self.tr(bt[:, k * 128:(k + 1) * 128],
                            wob[k][:, jc * 128:(jc + 1) * 128], self.ib[:],
                            w, bank.pi() if k == 3 else ())
                c.emit("scalar", (lambda h, o=wost[si][:], i=bt[:, 0:512]:
                                  h.copy(o, i)),
                       list(bank.cw()) + list(g_st[si].pw()), g_st[si].pi())
                bank.freed(g_st[si])
                if jc == 15:
                    for kk in range(4):
                        g_ld[kk].freed(g_st[si])
                c.dma("sync", self.wots[jc * 128:(jc + 1) * 128,
                                        rcb * 512:(rcb + 1) * 512],
                      wost[si][:], g_st[si].cw(), g_wd.pi(16))
                g_st[si].freed(g_wd)

    def phase_a1(self):
        c = self.ctx
        nc = self.nc
        NKT = self.NKT
        sb = lambda n, sh, ty: self.phstack.enter_context(nc.sbuf_tensor(n, sh, ty))
        wdkv_sb = sb("wdkv_sb", [128, 16 * KV], BF16)
        wkr_sb = sb("wkr_sb", [128, 16 * 256], BF16)
        xa = [sb(f"xa{i}", [128, D], F32) for i in range(2)]
        xab = [sb(f"xab{i}", [128, D], BF16) for i in range(2)]
        xT = sb("xT", [128, 16 * 512], BF16)
        ckvpre = [sb(f"ckvpre{i}", [128, KV], F32) for i in range(2)]
        ckvbf = [sb(f"ckvbf{i}", [128, KV], BF16) for i in range(2)]
        ckvTst = [sb(f"ckvTst{i}", [128, 512], BF16) for i in range(2)]
        ck_sb = sb("ck_sb", [128, 512], F32)
        sk_sb = sb("sk_sb", [128, 512], F32)
        krf = sb("krf", [128, 512], F32)
        krrot = sb("krrot", [128, 512], F32)
        krb = sb("krb", [128, 512], BF16)
        ksq1 = [sb(f"ksq1{i}", [128, 512], BF16) for i in range(2)]
        junk2 = sb("junk2", [128, KV], F32)
        kvlw_sb = sb("kvlw_sb", [128, KV], F32)
        kvlb_sb = sb("kvlb_sb", [128, KV], F32)

        g_w = Gate(c, "a1.w", kind="g")
        g_xa = [Gate(c, f"a1.xa{i}") for i in range(2)]
        g_xab = [Gate(c, f"a1.xab{i}") for i in range(2)]
        g_xT = Gate(c, "a1.xT")
        g_pre = [Gate(c, f"a1.pre{i}") for i in range(2)]
        g_ckv = [Gate(c, f"a1.ckv{i}") for i in range(2)]
        g_cst = [Gate(c, f"a1.cst{i}") for i in range(2)]
        g_cd = Gate(c, "a1.cd")
        g_rope = Gate(c, "a1.rope")
        g_krf = Gate(c, "a1.krf")
        g_krb = Gate(c, "a1.krb")
        g_kd = Gate(c, "a1.kd")
        g_kn1 = Gate(c, "a1.kn1")
        g_lvs2 = [Gate(c, f"a1.lvs{i}") for i in range(2)]
        g_lsv2 = [Gate(c, f"a1.lsv{i}") for i in range(2)]

        for dc in range(16):
            c.dma("gpsimd", wdkv_sb[:, dc * KV:(dc + 1) * KV],
                  self.wdkv[dc * 128:(dc + 1) * 128, :], (), g_w.pi(16),
                  first=dc == 0)
            c.dma("gpsimd", wkr_sb[:, dc * 256:(dc + 1) * 256],
                  self.wkr[dc * 128:(dc + 1) * 128, :], (), g_w.pi(16),
                  first=False)
        g_wl = Gate(c, "a1.wl")
        c.dma("sync", kvlw_sb[:], self.kvlw, (), g_wl.pi(16))
        c.dma("sync", kvlb_sb[:], self.kvlb, (), g_wl.pi(16), first=False)
        wready = list(g_w.cw()) + list(g_wl.cw())

        pts = self.pg[:6]
        trp = [self.pg[6], self.pg[7]]
        btr = [self.pb[6].bitcast(BF16), self.pb[7].bitcast(BF16)]
        trp4 = [self.pg[i] for i in range(4)]
        btr4 = [self.pb[i].bitcast(BF16) for i in range(4)]

        krn = [0]
        kr_pend = []
        kr_sqw = [None, None]

        def a1_krow(h_, kt_, oc_, sqw):
            ri = krn[0]
            krn[0] += 1
            pbg = self.pg[4 + ri % 2]
            pbv = self.pb[4 + ri % 2]
            self.mm(pbv[0:1, 0:512], self.ones_cb[:], ksq1[oc_][:],
                    True, True, list(pbg.pw()) + [sqw], pbg.pi())
            c.emit("scalar", (lambda h, o=self.krow[0:1, 0:512],
                              i=pbv[0:1, 0:512]: h.copy(o, i)),
                   list(pbg.cw()) + list(g_kn1.cw()), g_kn1.pi())
            pbg.freed(g_kn1)
            col = h_ * self.NKTA + kt_
            c.emit("vector", (lambda h, o=self.kmxp[0:1, col:col + 1],
                              i=self.krow[0:1, 0:512]:
                              h.reduce_max(o, i, axis=AX.X)),
                   g_kn1.cw(), g_kn1.pi())

        for kt in range(self.NKTA):
            c.dma("sync", ck_sb[:], self.coskT[:, kt * 512:(kt + 1) * 512],
                  g_rope.pw(), g_rope.pi(16))
            c.dma("sync", sk_sb[:], self.sinkT[:, kt * 512:(kt + 1) * 512],
                  (), g_rope.pi(16), first=False)
            for ch in range(4):
                if ch == 1 and kr_pend:
                    a1_krow(*kr_pend.pop(0))
                i = ch % 2
                c.dma("sync", xa[i][:],
                      self.xkv[kt * 512 + ch * 128: kt * 512 + (ch + 1) * 128, :],
                      g_xa[i].pw(), g_xa[i].pi(16))
                xabr = []
                for qb in range(4):
                    c.emit("vector", (lambda h,
                                      o=xab[i][:, qb * 512:(qb + 1) * 512],
                                      s=xa[i][:, qb * 512:(qb + 1) * 512]:
                                      h.tensor_copy(o, s)),
                           list(g_xa[i].cw()) +
                           (list(g_xab[i].pw()) if qb == 0 else []),
                           g_xab[i].pi())
                    xabr.append((g_xab[i].s, g_xab[i].s.n))
                g_xa[i].freed(g_xab[i])
                for db in range(4):
                    tg = trp4[db]
                    tv = btr4[db]
                    for k in range(4):
                        dc = db * 4 + k
                        w = []
                        if k == 0:
                            w = list(tg.pw()) + [xabr[db]]
                            if ch == 0:
                                w += list(g_xT.pw())
                        self.tr(tv[:, k * 128:(k + 1) * 128],
                                xab[i][:, dc * 128:(dc + 1) * 128], self.ib[:],
                                w, tg.pi() if k == 3 else ())
                    dst = xT[:].rearrange("p (dc t) -> p dc t", dc=16)[
                        :, db * 4:(db + 1) * 4, ch * 128:(ch + 1) * 128]
                    src = tv[:, 0:512].rearrange("p (b t) -> p b t", b=4)
                    c.emit("scalar", (lambda h, o=dst, s=src: h.copy(o, s)),
                           tg.cw(), g_xT.pi())
                    tg.freed(g_xT)
                    if db == 3:
                        g_xab[i].freed(g_xT)
            xtread = g_xT.cw()

            def a1_mms(ts):
                bset = pts[3 * (ts % 2): 3 * (ts % 2) + 3]
                for dc in range(16):
                    lhsT = xT[:, dc * 512 + ts * 128: dc * 512 + (ts + 1) * 128]
                    for si, (o0, o1) in enumerate(((0, 512), (512, 1024), (1024, KV))):
                        w = []
                        if dc == 0:
                            w = list(bset[si].pw())
                            if si == 0:
                                w += list(xtread) + list(wready)
                        self.mm(self.pb[3 * (ts % 2) + si][:, 0:o1 - o0],
                                lhsT,
                                wdkv_sb[:, dc * KV + o0: dc * KV + o1],
                                dc == 0, dc == 15, w,
                                bset[si].pi() if dc == 15 else ())

            def a1_post(ts, kt):
                ci = ts % 2
                bset = pts[3 * ci: 3 * ci + 3]
                for si, (o0, o1) in enumerate(((0, 512), (512, 1024), (1024, KV))):
                    bk = bset[si]
                    c.emit("scalar",
                           (lambda h, o=ckvpre[ci][:, o0:o1],
                            i=self.pb[3 * ci + si][:, 0:o1 - o0]:
                            h.copy(o, i)),
                           list(bk.cw()) + (list(g_pre[ci].pw()) if si == 0 else []),
                           g_pre[ci].pi())
                    bk.freed(g_pre[ci])
                c.emit("vector", None, list(g_ckv[ci].pw()), ())
                jn = self.junkA[:, 0:KV] if ci == 0 else junk2[:, 0:KV]
                self.ln_chain(ckvpre[ci][:], KV, ckvbf[ci][:], g_lvs2[ci],
                              g_lsv2[ci], g_ckv[ci], kvlw_sb[:], kvlb_sb[:],
                              jn, pre_gate=g_pre[ci],
                              pre_ready=g_pre[ci].cw(), stats_base=5 * ci,
                              junk=jn)
                for bi, grp in enumerate(((0, 1, 2, 3), (4, 5, 6, 7), (8, 9), (10,))):
                    si = bi % 2
                    for k, cc in enumerate(grp):
                        rows = kvrows(cc)
                        w = []
                        if k == 0:
                            w = list(trp[si].pw()) + list(g_ckv[ci].cw())
                        self.tr(btr[si][0:rows, k * 128:(k + 1) * 128],
                                ckvbf[ci][:, cc * 128:cc * 128 + rows],
                                self.ib[:],
                                w, trp[si].pi() if k == len(grp) - 1 else ())
                    rows = kvrows(grp[-1])
                    wid = 128 * len(grp)
                    c.emit("scalar",
                           (lambda h, o=ckvTst[si][0:128, 0:wid],
                            i=btr[si][:, 0:wid]: h.copy(o, i)),
                           list(trp[si].cw()) + list(g_cst[si].pw()),
                           g_cst[si].pi())
                    trp[si].freed(g_cst[si])
                    if bi == 3:
                        g_ckv[ci].freed(g_cst[si])
                    r0 = grp[0] * 128
                    col = kt * 512 + ts * 128
                    if rows != 128:
                        src = ckvTst[si][0:rows, 0:128]
                        dst = self.ckvts[r0:r0 + rows, col:col + 128]
                    else:
                        src = ckvTst[si][:].rearrange(
                            "p (b t) -> p b t", b=4)[:, 0:len(grp), 0:128]
                        dst = self.ckvts[r0:r0 + 128 * len(grp),
                                         col:col + 128].rearrange(
                            "(b p) t -> p b t", p=128)
                    c.dma("sync", dst, src, g_cst[si].cw(), g_cd.pi(16))
                    g_cst[si].freed(g_cd)

            for stage in range(5):
                if stage < 4:
                    a1_mms(stage)
                if stage >= 1:
                    a1_post(stage - 1, kt)
            for oc in range(2):
                bk = pts[oc]
                for dc in range(16):
                    lhsT = wkr_sb[:, dc * 256 + oc * 128: dc * 256 + (oc + 1) * 128]
                    w = list(bk.pw()) if dc == 0 else []
                    self.mm(self.pb[oc][:, 0:512], lhsT,
                            xT[:, dc * 512:(dc + 1) * 512],
                            dc == 0, dc == 15, w, bk.pi() if dc == 15 else ())
                if oc == 1:
                    g_xT.freed(bk)
                c.emit("scalar", (lambda h, o=krf[:], i=self.pb[oc][:, 0:512]:
                                  h.copy(o, i)),
                       list(bk.cw()) + list(g_krf.pw()), g_krf.pi())
                c.emit("scalar", (lambda h, o=ksq1[oc][:], i=krf[:]:
                                  h.activation(out=o, in_=i, func=AF.Square)),
                       g_krf.cw() + g_kn1.cw(), g_kn1.pi())
                kr_sqw[oc] = (g_kn1.s, g_kn1.s.n)
                g_krf.freed(g_kn1)
                c.emit("scalar", (lambda h, o=krrot[0:64, :], i=krf[64:128, :]:
                                  h.activation(out=o, in_=i, func=AF.Copy,
                                               scale=-1.0)),
                       g_krf.cw(), g_krf.pi())
                c.emit("scalar", (lambda h, o=krrot[64:128, :], i=krf[0:64, :]:
                                  h.copy(o, i)), g_krf.cw(), g_krf.pi())
                bk.freed(g_krf)
                if oc == 1:
                    a1_krow(14, kt, 0, kr_sqw[0])
                    kr_pend.append((15, kt, 1, kr_sqw[1]))
                c.emit("vector", (lambda h, o=self.junkB[:], a=krf[:], b=ck_sb[:]:
                                  h.tensor_mul(o, a, b)),
                       list(g_krf.cw()) + list(g_rope.cw()) + list(g_krb.pw()) +
                       list(g_krb.cw()), g_krb.pi())
                c.emit("vector", (lambda h, o=self.junkC[:], a=krrot[:], b=sk_sb[:]:
                                  h.tensor_mul(o, a, b)), g_krb.cw(), g_krb.pi())
                c.emit("vector", (lambda h, o=krb[:], a=self.junkB[:],
                                  b=self.junkC[:]: h.tensor_add(o, a, b)),
                       g_krb.cw(), g_krb.pi())
                g_krf.freed(g_krb)
                if oc == 1:
                    g_rope.freed(g_krb)
                c.dma("sync", self.ktsh[14 + oc, :, kt * 512:(kt + 1) * 512],
                      krb[:], g_krb.cw(), g_kd.pi(16))
                g_krb.freed(g_kd)
        while kr_pend:
            a1_krow(*kr_pend.pop(0))

    def phase_a2(self):
        c = self.ctx
        nc = self.nc
        NKT = self.NKT
        sb = lambda n, sh, ty: self.phstack.enter_context(nc.sbuf_tensor(n, sh, ty))
        wuk_sb = sb("wuk_sb", [128, KVC * 2048], BF16)
        wukK_sb = sb("wukK_sb", [128, KVC * 1792], BF16)
        ckvT_sb = [sb(f"ckvT_sb{i}", [128, KVC * 512], BF16) for i in range(2)]
        evst = [sb(f"evst{i}", [128, 512], BF16) for i in range(2)]
        ksq2 = sb("ksq2", [128, 512], BF16)
        g_wu = Gate(c, "a2.wu", kind="g")
        g_wuK = Gate(c, "a2.wuK", kind="g")
        g_kn2 = Gate(c, "a2.kn2")
        g_ct = [Gate(c, f"a2.ct{i}") for i in range(2)]
        g_ev = [Gate(c, f"a2.ev{i}") for i in range(2)]
        g_kvd = Gate(c, "a2.kvd")

        def load_ckvT(kt, ti):
            for cc in range(KVC):
                rows = kvrows(cc)
                c.dma("sync", ckvT_sb[ti][0:rows, cc * 512:(cc + 1) * 512],
                      self.ckvts[cc * 128:cc * 128 + rows,
                                 kt * 512:(kt + 1) * 512],
                      g_ct[ti].pw() if cc == 0 else (), g_ct[ti].pi(16),
                      first=cc == 0)

        ksqs = [ksq2, self.phstack.enter_context(
            nc.sbuf_tensor("ksq2b", [128, 512], BF16))]
        pend_kn = []

        def knorm_sq(si2, src_view, src_wait, src_freed_gate):
            c.emit("scalar", (lambda h, o=ksqs[si2][:], i=src_view:
                              h.activation(out=o, in_=i, func=AF.Square)),
                   tuple(src_wait) + g_kn2.cw(), g_kn2.pi())
            src_freed_gate.freed(g_kn2)
            return (g_kn2.s, g_kn2.s.n)

        def knorm_row(h_, kt_, si2, ri, sqw):
            pbg = self.pg[4 + ri % 2]
            pbv = self.pb[4 + ri % 2]
            self.mm(pbv[0:1, 0:512], self.ones_cb[:], ksqs[si2][:],
                    True, True, list(pbg.pw()) + [sqw],
                    pbg.pi())
            c.emit("scalar", (lambda h, o=self.krow[0:1, 0:512],
                              i=pbv[0:1, 0:512]: h.copy(o, i)),
                   list(pbg.cw()) + list(g_kn2.cw()), g_kn2.pi())
            pbg.freed(g_kn2)
            col = h_ * self.NKTA + kt_
            c.emit("vector", (lambda h, o=self.kmxp[0:1, col:col + 1],
                              i=self.krow[0:1, 0:512]:
                              h.reduce_max(o, i, axis=AX.X)),
                   g_kn2.cw(), g_kn2.pi())

        groups = [[0, 1], [2, 3], [4, 5], [6, 7]]
        g_cc = self.cc_gate

        for cc in range(KVC):
            rows = kvrows(cc)
            c.dma("gpsimd", wuk_sb[0:rows, cc * 2048:(cc + 1) * 2048],
                  self.wukv[cc * 128:cc * 128 + rows, 1792:3840],
                  g_wu.pw() if cc == 0 else (), g_wu.pi(16), first=cc == 0)
        wur = g_wu.cw()
        for cc in range(KVC):
            rows = kvrows(cc)
            c.dma("gpsimd", wukK_sb[0:rows, cc * 1792:(cc + 1) * 1792],
                  self.wukv[cc * 128:cc * 128 + rows, 0:1792],
                  g_wuK.pw() if cc == 0 else (), g_wuK.pi(16), first=cc == 0)
        wurK = g_wuK.cw()
        for kt in range(self.NKTA):
            ti = kt % 2
            load_ckvT(kt, ti)
            for g in range(16):
                kc, vg = divmod(g, 4)
                bk = self.pg[g % 4]
                for cc in range(KVC):
                    rows = kvrows(cc)
                    w = []
                    if cc == 0:
                        w = list(bk.pw()) + list(g_ct[ti].cw()) + list(wur)
                    self.mm(self.pb[g % 4][:, 0:512],
                            ckvT_sb[ti][0:rows, cc * 512 + kc * 128:
                                        cc * 512 + (kc + 1) * 128],
                            wuk_sb[0:rows, cc * 2048 + vg * 512:
                                   cc * 2048 + (vg + 1) * 512],
                            cc == 0, cc == KVC - 1, w,
                            bk.pi() if cc == KVC - 1 else ())
                if g == 15:
                    g_ct[ti].freed(bk)
                    if kt == self.NKTA - 1:
                        g_wu.freed(bk)
                si = g % 2
                c.emit("scalar", (lambda h, o=evst[si][:],
                                  i=self.pb[g % 4][:, 0:512]: h.copy(o, i)),
                       list(bk.cw()) + list(g_ev[si].pw()), g_ev[si].pi())
                bk.freed(g_ev[si])
                c.dma("sync",
                      self.vrowsh[kt * 512 + kc * 128: kt * 512 + (kc + 1) * 128,
                                 vg * 512:(vg + 1) * 512],
                      evst[si][:], g_ev[si].cw(), g_kvd.pi(16))
                g_ev[si].freed(g_kvd)
        c.emit("gpsimd",
               (lambda h, gr=groups:
                h.collective_compute("AllGather", ALU.bypass,
                                     replica_groups=gr,
                                     ins=[self.vrowsh], outs=[self.vrows2])),
               list(g_kvd.cw()), g_cc.pi(), track=False)
        self.ccw_v = [(g_cc.s, g_cc.s.n)]

        rown = 0
        for kt in range(self.NKTA):
            ti = kt % 2
            load_ckvT(kt, ti)
            for ocl in range(14):
                bk = self.pg[ocl % 4]
                for cc in range(KVC):
                    rows = kvrows(cc)
                    w = []
                    if cc == 0:
                        w = list(bk.pw()) + list(g_ct[ti].cw()) + list(wurK)
                    self.mm(self.pb[ocl % 4][:, 0:512],
                            wukK_sb[0:rows, cc * 1792 + ocl * 128:
                                    cc * 1792 + (ocl + 1) * 128],
                            ckvT_sb[ti][0:rows, cc * 512:(cc + 1) * 512],
                            cc == 0, cc == KVC - 1, w,
                            bk.pi() if cc == KVC - 1 else ())
                if ocl == 13:
                    g_ct[ti].freed(bk)
                    if kt == self.NKTA - 1:
                        g_wuK.freed(bk)
                si = ocl % 2
                c.emit("scalar", (lambda h, o=evst[si][:],
                                  i=self.pb[ocl % 4][:, 0:512]: h.copy(o, i)),
                       list(bk.cw()) + list(g_ev[si].pw()), g_ev[si].pi())
                bk.freed(g_ev[si])
                c.dma("sync", self.ktsh[ocl, :, kt * 512:(kt + 1) * 512],
                      evst[si][:], g_ev[si].cw(), g_kvd.pi(16))
                g_ev[si].freed(g_kvd)
                sqw = knorm_sq(ocl % 2, evst[si][:], g_ev[si].cw(), g_ev[si])
                if pend_kn:
                    h_p, kt_p, si_p, w_p = pend_kn.pop(0)
                    knorm_row(h_p, kt_p, si_p, rown, w_p)
                    rown += 1
                pend_kn.append((ocl, kt, ocl % 2, sqw))
        while pend_kn:
            h_p, kt_p, si_p, w_p = pend_kn.pop(0)
            knorm_row(h_p, kt_p, si_p, rown, w_p)
            rown += 1
        c.dma("sync", self.kmsh, self.kmxp[:], g_kn2.cw(), g_kn2.pi(16))

    def emit_cc_kms(self):
        c = self.ctx
        groups = [[0, 1], [2, 3], [4, 5], [6, 7]]
        g_cc = self.cc_gate
        c.emit("gpsimd",
               (lambda h, gr=groups:
                h.collective_compute("AllGather", ALU.bypass,
                                     replica_groups=gr,
                                     ins=[self.kmsh], outs=[self.kms2])),
               (), g_cc.pi(), track=False)
        self.ccw_m = [(g_cc.s, g_cc.s.n)]
        self.ccw_g = []

    def emit_cc_kts(self, gi):
        c = self.ctx
        groups = [[0, 1], [2, 3], [4, 5], [6, 7]]
        g_cc = self.cc_gate
        c.emit("gpsimd",
               (lambda h, gr=groups,
                i=self.ktsh[gi * 4:(gi + 1) * 4],
                o=self.kts2g[gi]:
                h.collective_compute("AllGather", ALU.bypass,
                                     replica_groups=gr,
                                     ins=[i], outs=[o])),
               (), g_cc.pi(), track=False)
        self.ccw_g.append([(g_cc.s, g_cc.s.n)])

    def phase_b(self):
        c = self.ctx
        nc = self.nc
        NQB = self.NQB
        sb = lambda n, sh, ty: self.phstack.enter_context(nc.sbuf_tensor(n, sh, ty))
        qlw_sb = sb("qlw_sb", [128, QP], F32)
        qlb_sb = sb("qlb_sb", [128, QP], F32)
        xa2 = [sb(f"xa2_{i}", [128, D], F32) for i in range(2)]
        xb2 = sb("xb2", [128, D], BF16)
        xTq = sb("xTq", [128, 16 * 512], BF16)
        wstr = [sb(f"wstr{i}", [128, 1024], BF16) for i in range(2)]
        cqf = sb("cqf", [128, QP], F32)
        cqbf = sb("cqbf", [128, QP], BF16)
        cqT = sb("cqT", [128, 8 * 512], BF16)
        QT = sb("QT", [128, 16 * 512], BF16)
        U = sb("U", [128, 69632], U8)
        ub = U[:].bitcast(BF16)
        uf = U[:].bitcast(F32)
        kh = [ub[:, 0:4096], ub[:, 4096:8192]]
        vh = [ub[:, 8192:12288], ub[:, 12288:16384]]
        pts4 = [ub[:, 24576 + 512 * i:25088 + 512 * i] for i in range(4)]
        mbt = ub[:, 26624:30720]
        x1pre = uf[:, 0:8192]
        xqr = [uf[:, 8192:10240], uf[:, 10240:12288]]
        qrf = uf[:, 15360:15872]
        qrrot = uf[:, 15872:16384]
        cq_ct = uf[:, 16384:16896]
        cq_st = uf[:, 16896:17408]
        rlbc = sb("rlbc", [128, 512], F32)
        kmxB = self.phstack.enter_context(
            nc.sbuf_tensor("kmxB", [1, NH * self.NKT], F32))
        nch_sb = sb("nch_sb", [128, NH], F32)
        junkQ = sb("junkQ", [128, 512], BF16)
        lrow_sb = self.phstack.enter_context(
            nc.sbuf_tensor("lrow_sb", [1, 1024], F32))
        aoT = sb("aoT", [128, 16 * 512], BF16)
        wotr = [sb(f"wotr{i}", [128, 1024], BF16) for i in range(2)]
        x1Tst = [sb(f"x1Tst{i}", [128, 512], BF16) for i in range(2)]

        self.emit_cc_kms()
        self.emit_cc_kts(0)

        g_pre = Gate(c, "b.pre")
        c.dma("sync", mbt, self.maskst, (), g_pre.pi(16))
        c.dma("sync", qlw_sb[:], self.qlw, (), g_pre.pi(16), first=False)
        c.dma("sync", qlb_sb[:], self.qlb, (), g_pre.pi(16), first=False)
        NKTA = self.NKTA
        for half in range(2):
            dst = kmxB[0:1, :].rearrange("o (h k) -> o h k", k=self.NKT)[
                :, :, half * NKTA:(half + 1) * NKTA]
            src = self.kms2[half:half + 1, :].rearrange(
                "o (h k) -> o h k", k=NKTA)
            c.dma("sync", dst, src, list(self.ccw_m) if half == 0 else (),
                  g_pre.pi(16), first=half == 0, slow=True)
        pre_r = g_pre.cw()

        g_xa = [Gate(c, f"b.xa{i}") for i in range(2)]
        g_xb = Gate(c, "b.xb")
        g_xTq = Gate(c, "b.xTq")
        g_ws = [Gate(c, f"b.ws{i}", kind="g") for i in range(2)]
        g_cqf = Gate(c, "b.cqf")
        g_cqbf = Gate(c, "b.cqbf")
        g_cqT = Gate(c, "b.cqT")
        g_QT = Gate(c, "b.QT")
        g_qrf = Gate(c, "b.qrf")
        g_rope = Gate(c, "b.rope")
        g_kh = [Gate(c, f"b.kh{i}", kind="g") for i in range(2)]
        g_vh = [Gate(c, f"b.vh{i}", kind="g") for i in range(2)]
        g_pt = [Gate(c, f"b.pt{i}") for i in range(4)]
        g_l6 = [Gate(c, f"b.l6{i}") for i in range(2)]
        g_nch = Gate(c, "b.nch")
        g_rls = Gate(c, "b.rls")
        g_rlb = Gate(c, "b.rlb")
        g_aoT = Gate(c, "b.aoT")
        g_wot = [Gate(c, f"b.wot{i}") for i in range(2)]
        g_x1p = Gate(c, "b.x1p")
        g_xqr = [Gate(c, f"b.xqr{i}") for i in range(2)]
        g_x1f = Gate(c, "b.x1f")
        g_x1T = [Gate(c, f"b.x1T{i}") for i in range(2)]
        g_x1d = Gate(c, "b.x1d")
        g_lvs = Gate(c, "b.lvs")
        g_lsv = Gate(c, "b.lsv")
        bendw = []
        prev_aor = ()
        g_dcp = Gate(c, "b.dcp")
        g_icp = Gate(c, "b.icp")
        g_avp = Gate(c, "b.avp")

        ptrb = self.pg[7]
        btr = self.pb[7].bitcast(BF16)

        for n in range(NQB):
            nkt = self.nkt_of(n)
            nk = nkt * 512
            nkc = nk // 128
            c.dma("sync", cq_ct, self.cosqT[:, n * 512:(n + 1) * 512],
                  list(g_rope.pw()) + list(prev_aor), g_rope.pi(16))
            c.dma("sync", cq_st, self.sinqT[:, n * 512:(n + 1) * 512],
                  (), g_rope.pi(16), first=False)
            for ch in range(4):
                i = ch % 2
                c.dma("sync", xa2[i][:],
                      self.xq[n * 512 + ch * 128: n * 512 + (ch + 1) * 128, :],
                      g_xa[i].pw(), g_xa[i].pi(16))
                c.emit("vector", (lambda h, o=xb2[:], s=xa2[i][:]:
                                  h.tensor_copy(o, s)),
                       list(g_xa[i].cw()) + list(g_xb.pw()), g_xb.pi())
                g_xa[i].freed(g_xb)
                for db in range(4):
                    for k in range(4):
                        dc = db * 4 + k
                        w = []
                        if k == 0:
                            w = list(ptrb.pw()) + list(g_xb.cw())
                            if ch == 0 and db == 0:
                                w += list(g_xTq.pw())
                        self.tr(btr[:, k * 128:(k + 1) * 128],
                                xb2[:, dc * 128:(dc + 1) * 128], self.ib[:],
                                w, ptrb.pi() if k == 3 else ())
                    dst = xTq[:].rearrange("p (dc t) -> p dc t", dc=16)[
                        :, db * 4:(db + 1) * 4, ch * 128:(ch + 1) * 128]
                    src = btr[:, 0:512].rearrange("p (b t) -> p b t", b=4)
                    c.emit("scalar", (lambda h, o=dst, s=src: h.copy(o, s)),
                           ptrb.cw(), g_xTq.pi())
                    ptrb.freed(g_xTq)
                    if db == 3:
                        g_xb.freed(g_xTq)
            xtr = g_xTq.cw()
            for dc in range(16):
                i = dc % 2
                c.dma("gpsimd", wstr[i][:],
                      self.wdq[dc * 128:(dc + 1) * 128, :],
                      g_ws[i].pw(), g_ws[i].pi(16))
                for ts in range(4):
                    lhsT = xTq[:, dc * 512 + ts * 128: dc * 512 + (ts + 1) * 128]
                    for si in range(2):
                        bk = self.pg[2 * ts + si]
                        w = []
                        if dc == 0:
                            w = list(bk.pw())
                            if ts == 0 and si == 0:
                                w += list(xtr) + list(pre_r)
                        if si == 0:
                            w = w + list(g_ws[i].cw())
                        last = ts == 3 and si == 1
                        inc = bk.pi() if dc == 15 else (g_dcp.pi() if last else ())
                        self.mm(self.pb[2 * ts + si][:, 0:512], lhsT,
                                wstr[i][:, si * 512:(si + 1) * 512],
                                dc == 0, dc == 15, w, inc)
                        if last:
                            g_ws[i].freed(g_dcp if dc < 15 else bk)
                            if dc == 15:
                                g_xTq.freed(bk)
            if n == 0:
                self.emit_cc_kts(1)
            for ts in range(4):
                for si in range(2):
                    bk = self.pg[2 * ts + si]
                    c.emit("scalar", (lambda h, o=cqf[:, si * 512:(si + 1) * 512],
                                      i=self.pb[2 * ts + si][:, 0:512]:
                                      h.copy(o, i)),
                           list(bk.cw()) + (list(g_cqf.pw()) if si == 0 else []),
                           g_cqf.pi())
                    bk.freed(g_cqf)
                c.emit("vector", None, list(g_cqbf.pw()), ())
                self.ln_chain(cqf[:], QP, cqbf[:], g_lvs, g_lsv, g_cqbf,
                              qlw_sb[:], qlb_sb[:], self.junkA[:, 0:QP],
                              pre_gate=g_cqf, pre_ready=g_cqf.cw())
                for db in range(2):
                    for k in range(4):
                        cc = db * 4 + k
                        w = []
                        if k == 0:
                            w = list(ptrb.pw()) + list(g_cqbf.cw())
                        self.tr(btr[:, k * 128:(k + 1) * 128],
                                cqbf[:, cc * 128:(cc + 1) * 128], self.ib[:],
                                w, ptrb.pi() if k == 3 else ())
                    dst = cqT[:].rearrange("p (cc t) -> p cc t", cc=8)[
                        :, db * 4:(db + 1) * 4, ts * 128:(ts + 1) * 128]
                    src = btr[:, 0:512].rearrange("p (b t) -> p b t", b=4)
                    c.emit("scalar", (lambda h, o=dst, s=src: h.copy(o, s)),
                           list(ptrb.cw()) +
                           (list(g_cqT.pw()) if ts == 0 and db == 0 else []),
                           g_cqT.pi())
                    ptrb.freed(g_cqT)
                    if db == 1:
                        g_cqbf.freed(g_cqT)
            cqtr = g_cqT.cw()
            for oh in range(2):
                for cqc in range(8):
                    i = cqc % 2
                    c.dma("gpsimd", wstr[i][:],
                          self.wqcat[cqc * 128:(cqc + 1) * 128,
                                     oh * 1024:(oh + 1) * 1024],
                          g_ws[i].pw(), g_ws[i].pi(16))
                    for ocl in range(8):
                        bk = self.pg[ocl]
                        w = []
                        if cqc == 0:
                            w = list(bk.pw())
                            if ocl == 0:
                                w += list(cqtr)
                        if ocl == 0:
                            w = w + list(g_ws[i].cw())
                        inc = bk.pi() if cqc == 7 else (g_dcp.pi() if ocl == 7 else ())
                        self.mm(self.pb[ocl][:, 0:512],
                                wstr[i][:, ocl * 128:(ocl + 1) * 128],
                                cqT[:, cqc * 512:(cqc + 1) * 512],
                                cqc == 0, cqc == 7, w, inc)
                        if ocl == 7:
                            g_ws[i].freed(g_dcp if cqc < 7 else bk)
                            if cqc == 7 and oh == 1:
                                g_cqT.freed(bk)
                for ocl in range(8):
                    oc = oh * 8 + ocl
                    bk = self.pg[ocl]
                    if oc < 14:
                        c.emit("scalar",
                               (lambda h, o=QT[:, oc * 512:(oc + 1) * 512],
                                i=self.pb[ocl][:, 0:512]:
                                h.activation(out=o, in_=i, func=AF.Copy,
                                             scale=SCALE)),
                               list(bk.cw()) +
                               (list(g_QT.pw()) if oc == 0 else []),
                               g_QT.pi())
                        bk.freed(g_QT)
                    else:
                        c.emit("scalar", (lambda h, o=qrf,
                                          i=self.pb[ocl][:, 0:512]:
                                          h.activation(out=o, in_=i, func=AF.Copy,
                                                       scale=SCALE)),
                               list(bk.cw()) + list(g_qrf.pw()), g_qrf.pi())
                        c.emit("scalar", (lambda h, o=qrrot[0:64, :],
                                          i=qrf[64:128, :]:
                                          h.activation(out=o, in_=i, func=AF.Copy,
                                                       scale=-1.0)),
                               g_qrf.cw(), g_qrf.pi())
                        c.emit("scalar", (lambda h, o=qrrot[64:128, :],
                                          i=qrf[0:64, :]:
                                          h.copy(o, i)), g_qrf.cw(), g_qrf.pi())
                        bk.freed(g_qrf)
                        c.emit("vector", (lambda h, o=self.junkB[:], a=qrf,
                                          b=cq_ct: h.tensor_mul(o, a, b)),
                               list(g_qrf.cw()) + list(g_rope.cw()) +
                               list(g_QT.cw()), g_QT.pi())
                        c.emit("vector", (lambda h, o=self.junkC[:], a=qrrot[:],
                                          b=cq_st: h.tensor_mul(o, a, b)),
                               g_QT.cw(), g_QT.pi())
                        c.emit("vector",
                               (lambda h, o=QT[:, oc * 512:(oc + 1) * 512],
                                a=self.junkB[:], b=self.junkC[:]:
                                h.tensor_add(o, a, b)), g_QT.cw(), g_QT.pi())
                        g_qrf.freed(g_QT)
                        if oc == 15:
                            g_rope.freed(g_QT)
                if n == 0:
                    self.emit_cc_kts(2 + oh)
            qtread = g_QT.cw()

            slots = [self.pg[0], self.pg[1], self.pg[2], self.pg[3],
                     self.pg[7]]
            svs = [self.pb[0], self.pb[1], self.pb[2], self.pb[3],
                   self.pb[7]]
            pOs = [self.pg[4], self.pg[5]]
            sg = [0]

            def slot():
                s = sg[0] % 5
                sg[0] += 1
                return slots[s], svs[s]

            h6w = list(self.pg[6].pw())
            nch_ready = {}
            junkB = self.junkB
            junkC_r = self.junkC

            def emit_nch_a(h_):
                hp = h_ % 2
                nrow = (junkB if hp == 0 else junkC_r)[0:1, 0:512]
                kb = 512 + 4 * hp
                sl, sv = slot()
                c.emit("scalar", (lambda h, o=junkQ[:],
                                  i=QT[:, h_ * 512:(h_ + 1) * 512]:
                                  h.activation(out=o, in_=i, func=AF.Square)),
                       list(g_QT.cw()) + list(g_nch.cw()), g_nch.pi())
                self.mm(sv[0:1, 0:512], self.ones_cb[:], junkQ[:],
                        True, True, list(sl.pw()) + list(g_nch.cw()),
                        sl.pi())
                c.emit("scalar", (lambda h, o=nrow,
                                  i=sv[0:1, 0:512]: h.copy(o, i)),
                       list(sl.cw()) + list(g_nch.cw()), g_nch.pi())
                sl.freed(g_nch)
                c.emit("vector", (lambda h, o=self.krow[0:1, kb:kb + 1],
                                  i=nrow:
                                  h.reduce_max(o, i, axis=AX.X)),
                       g_nch.cw(), g_nch.pi())
                kc0 = h_ * self.NKT
                c.emit("vector", (lambda h, o=self.krow[0:1, kb + 3:kb + 4],
                                  i=kmxB[0:1, kc0:kc0 + nkt]:
                                  h.reduce_max(o, i, axis=AX.X)),
                       list(g_nch.cw()) + list(pre_r), g_nch.pi())
                c.emit("vector", (lambda h, o=self.krow[0:1, kb + 1:kb + 2],
                                  i=self.krow[0:1, kb:kb + 1],
                                  s1=self.krow[0:1, kb + 3:kb + 4]:
                                  h.tensor_scalar(o, i, s1, 1.0,
                                                  op0=ALU.mult, op1=ALU.mult)),
                       g_nch.cw(), g_nch.pi())
                c.emit("scalar", (lambda h, o=self.krow[0:1, kb + 2:kb + 3],
                                  i=self.krow[0:1, kb + 1:kb + 2]:
                                  h.activation(out=o, in_=i, func=AF.Sqrt)),
                       g_nch.cw(), g_nch.pi())

            def emit_nch_b(h_):
                hp = h_ % 2
                kb = 512 + 4 * hp
                sl2, sv2 = slot()
                self.mm(sv2[:, 0:1], self.ones[:],
                        self.krow[0:1, kb + 2:kb + 3],
                        True, True, list(sl2.pw()) + list(g_nch.cw()),
                        sl2.pi())
                c.emit("scalar", (lambda h, o=nch_sb[:, h_:h_ + 1],
                                  i=sv2[:, 0:1]:
                                  h.activation(out=o, in_=i, func=AF.Copy,
                                               scale=-1.0)),
                       list(sl2.cw()) + list(g_nch.cw()), g_nch.pi())
                sl2.freed(g_nch)
                nch_ready[h_] = (g_nch.s, g_nch.s.n)

            def emit_fills(h_):
                hb = h_ % 2
                bw = list(bendw) if h_ < 2 else []
                bwv = list(bw)
                if n == 0:
                    bw = bw + list(self.ccw_g[h_ // 4])
                    if h_ < 2:
                        bwv = bwv + list(self.ccw_v)
                SH = self.SH
                n0 = min(nk, SH)
                ktg = self.kts2g[h_ // 4]
                hsub = h_ % 4
                c.dma("gpsimd", kh[hb][:, 0:n0], ktg[0, hsub, :, 0:n0],
                      list(g_kh[hb].pw()) + bw, g_kh[hb].pi(16))
                if nk > SH:
                    c.dma("gpsimd", kh[hb][:, SH:nk],
                          ktg[1, hsub, :, 0:nk - SH],
                          (), g_kh[hb].pi(16), first=False)
                c.dma("gpsimd",
                      vh[hb][:, 0:n0].rearrange("p (ck d) -> p ck d", d=128),
                      self.vrows2[0, 0:n0, h_ * 128:(h_ + 1) * 128].rearrange(
                          "(ck kp) d -> kp ck d", kp=128),
                      list(g_vh[hb].pw()) + bwv, g_vh[hb].pi(16))
                if nk > SH:
                    c.dma("gpsimd",
                          vh[hb][:, SH:nk].rearrange(
                              "p (ck d) -> p ck d", d=128),
                          self.vrows2[1, 0:nk - SH,
                                      h_ * 128:(h_ + 1) * 128].rearrange(
                              "(ck kp) d -> kp ck d", kp=128),
                          (), g_vh[hb].pi(16), first=False)

            pti = [0]

            def emit_st(h_, cc):
                hb = h_ % 2
                sl, sv = slot()
                w = list(sl.pw())
                if cc == 0:
                    w += list(g_kh[hb].cw())
                    if h_ == 0:
                        w += list(qtread)
                self.mm(sv[:, 0:512], kh[hb][:, cc * 128:(cc + 1) * 128],
                        QT[:, h_ * 512:(h_ + 1) * 512], True, True, w,
                        sl.pi())
                if cc == nkc - 1:
                    g_kh[hb].freed(sl)
                    if h_ == 15:
                        g_QT.freed(sl)
                p = pti[0] % 4
                pti[0] += 1
                pt = g_pt[p]
                pv = pts4[p]
                c.emit("scalar", (lambda h, o=pv, i=sv[:, 0:512],
                                  bb=nch_sb[:, h_:h_ + 1]:
                                  h.activation(out=o, in_=i, func=AF.Exp,
                                               bias=bb, scale=1.0)),
                       list(sl.cw()) + list(pt.pw()) + [nch_ready[h_]],
                       pt.pi())
                sl.freed(pt)
                if cc >= nkc - 8:
                    moff = (cc - (nkc - 8)) * 512
                    mw = list(pt.cw())
                    if h_ == 0 and n == 0 and cc == nkc - 8:
                        mw += list(pre_r)
                    c.emit("vector", (lambda h, o=pv, a=pv,
                                      m=mbt[:, moff:moff + 512]:
                                      h.tensor_mul(o, a, m)), mw, pt.pi())
                return p

            def emit_avl(h_, cc, p):
                hp = h_ % 2
                pO = pOs[hp]
                pt = g_pt[p]
                pv = pts4[p]
                w = list(pt.cw())
                if cc == 0:
                    w += list(pO.pw()) + list(g_vh[hp].cw())
                self.mm(self.pb[4 + hp][:, 0:512],
                        vh[hp][:, cc * 128:(cc + 1) * 128], pv,
                        cc == 0, cc == nkc - 1, w,
                        pO.pi() if cc == nkc - 1 else g_avp.pi())
                lw = list(pt.cw())
                pp = 32 * hp
                if cc == 0:
                    lw += list(g_l6[hp].pw())
                    if h_ < 2:
                        lw += h6w
                self.mm(self.pb[6][pp:pp + 1, 0:512], self.ones_cb[:], pv,
                        cc == 0, cc == nkc - 1, lw,
                        g_l6[hp].pi() if cc == nkc - 1 else g_avp.pi())
                pt.freed(g_l6[hp] if cc == nkc - 1 else g_avp)
                if cc == nkc - 1:
                    g_vh[hp].freed(pO)

            def emit_rl_a(h_):
                hp = h_ % 2
                pp = 32 * hp
                c.emit("scalar", (lambda h, o=lrow_sb[0:1, 0:512],
                                  i=self.pb[6][pp:pp + 1, 0:512]: h.copy(o, i)),
                       list(g_l6[hp].cw()) + list(g_rls.pw()), g_rls.pi())
                g_l6[hp].freed(g_rls)
                self.pg[6].freed(g_rls)
                c.emit("scalar", (lambda h, o=lrow_sb[0:1, 512:1024], i=lrow_sb[0:1, 0:512]:
                                  h.activation(out=o, in_=i, func=AF.Ln)),
                       g_rls.cw(), g_rls.pi())
                c.emit("scalar", (lambda h, o=lrow_sb[0:1, 0:512],
                                  i=lrow_sb[0:1, 512:1024]:
                                  h.activation(out=o, in_=i, func=AF.Exp,
                                               scale=-1.0)),
                       g_rls.cw(), g_rls.pi())

            def emit_rl_b(h_):
                hp = h_ % 2
                pO = pOs[hp]
                sl, sv = slot()
                self.mm(sv[:, 0:512], self.ones[:],
                        lrow_sb[0:1, 0:512], True, True,
                        list(sl.pw()) + list(g_rls.cw()), sl.pi())
                g_rls.freed(sl)
                c.emit("scalar", (lambda h, o=rlbc[:], i=sv[:, 0:512]:
                                  h.copy(o, i)),
                       list(sl.cw()) + list(g_rlb.pw()), g_rlb.pi())
                sl.freed(g_rlb)
                c.emit("vector", (lambda h, o=aoT[:, h_ * 512:(h_ + 1) * 512],
                                  a=self.pb[4 + hp][:, 0:512], b=rlbc[:]:
                                  h.tensor_mul(o, a, b)),
                       list(pO.cw()) + list(g_rlb.cw()) +
                       (list(g_aoT.pw()) if h_ == 0 else []),
                       g_aoT.pi())
                pO.freed(g_aoT)
                g_rlb.freed(g_aoT)

            LAG = 4
            emit_nch_a(0)
            if NH > 1:
                emit_nch_a(1)
            emit_nch_b(0)
            if NH > 1:
                emit_nch_b(1)
            emit_fills(0)
            q = []
            pend_rl = None
            for h_ in range(NH):
                for cc in range(nkc):
                    if len(q) >= LAG:
                        ph, pc, pp_ = q.pop(0)
                        emit_avl(ph, pc, pp_)
                        if pc == nkc - 1:
                            pend_rl = ph
                    q.append((h_, cc, emit_st(h_, cc)))
                    if cc == 0 and h_ + 2 < NH:
                        emit_nch_a(h_ + 2)
                    if cc == 3 and h_ + 1 < NH:
                        emit_fills(h_ + 1)
                    if cc == 4 and pend_rl is not None:
                        emit_rl_a(pend_rl)
                    if cc == 5 and h_ + 2 < NH:
                        emit_nch_b(h_ + 2)
                    if cc == 7 and pend_rl is not None:
                        emit_rl_b(pend_rl)
                        pend_rl = None
            while q:
                ph, pc, pp_ = q.pop(0)
                emit_avl(ph, pc, pp_)
            if pend_rl is not None:
                emit_rl_a(pend_rl)
                emit_rl_b(pend_rl)
            emit_rl_a(NH - 1)
            emit_rl_b(NH - 1)
            aor = g_aoT.cw()
            prev_aor = aor
            for jh in range(2):
                for ic in range(16):
                    i = ic % 2
                    c.dma("sync", wotr[i][:],
                          self.wots[ic * 128:(ic + 1) * 128,
                                    jh * 1024:(jh + 1) * 1024],
                          g_wot[i].pw(), g_wot[i].pi(16))
                    for s_ in range(4):
                        for si in range(2):
                            bk = self.pg[2 * s_ + si]
                            w = []
                            if ic == 0:
                                w = list(bk.pw())
                                if s_ == 0 and si == 0:
                                    w += list(aor)
                            if si == 0:
                                w = w + list(g_wot[i].cw())
                            last = s_ == 3 and si == 1
                            inc = bk.pi() if ic == 15 else (g_icp.pi() if last else ())
                            self.mm(self.pb[2 * s_ + si][:, 0:512],
                                    aoT[:, ic * 512 + s_ * 128:
                                        ic * 512 + (s_ + 1) * 128],
                                    wotr[i][:, si * 512:(si + 1) * 512],
                                    ic == 0, ic == 15, w, inc)
                            if last:
                                g_wot[i].freed(g_icp if ic < 15 else bk)
                                if jh == 1 and ic == 15:
                                    g_aoT.freed(bk)
                for s_ in range(4):
                    for si in range(2):
                        bk = self.pg[2 * s_ + si]
                        dst = x1pre[:, s_ * D + jh * 1024 + si * 512:
                                    s_ * D + jh * 1024 + (si + 1) * 512]
                        c.emit("scalar", (lambda h, o=dst,
                                          i=self.pb[2 * s_ + si][:, 0:512]:
                                          h.copy(o, i)),
                               list(bk.cw()) +
                               (list(g_x1p.pw())
                                if (jh == 0 and s_ == 0 and si == 0) else []),
                               g_x1p.pi())
                        bk.freed(g_x1p)
            for s_ in range(4):
                i = s_ % 2
                c.dma("sync", xqr[i],
                      self.xq[n * 512 + s_ * 128: n * 512 + (s_ + 1) * 128, :],
                      list(g_xqr[i].pw()) + list(aor), g_xqr[i].pi(16))
                c.emit("vector", (lambda h, o=x1pre[:, s_ * D:(s_ + 1) * D],
                                  a=x1pre[:, s_ * D:(s_ + 1) * D], b=xqr[i]:
                                  h.tensor_add(o, a, b)),
                       list(g_xqr[i].cw()) + list(g_x1p.cw()), g_x1p.pi())
                g_xqr[i].freed(g_x1p)
                x1v = x1pre[:, s_ * D:(s_ + 1) * D]
                c.emit("vector", None, g_x1f.pw(), ())
                self.ln_chain(x1v, D, x1v,
                              g_lvs, g_lsv, g_x1f, pre_gate=g_x1p,
                              pre_ready=g_x1p.cw())
                c.dma("sync",
                      self.x1s[n * 512 + s_ * 128: n * 512 + (s_ + 1) * 128, :],
                      x1v, g_x1f.cw(), g_x1d.pi(16))
                for db in range(4):
                    si = db % 2
                    for k in range(4):
                        hc = db * 4 + k
                        w = []
                        if k == 0:
                            w = list(ptrb.pw()) + list(g_x1f.cw())
                        self.tr(self.pb[7][:, k * 128:(k + 1) * 128],
                                x1v[:, hc * 128:(hc + 1) * 128], self.iff[:],
                                w, ptrb.pi() if k == 3 else ())
                    c.emit("scalar", (lambda h, o=x1Tst[si][:],
                                      i=self.pb[7][:, 0:512]:
                                      h.copy(o, i)),
                           list(ptrb.cw()) + list(g_x1T[si].pw()),
                           g_x1T[si].pi())
                    ptrb.freed(g_x1T[si])
                    if db == 3:
                        g_x1f.freed(g_x1T[si])
                    col = n * 512 + s_ * 128
                    for k in range(4):
                        hc = db * 4 + k
                        c.dma("sync", self.x1ts[hc * 128:(hc + 1) * 128,
                                                col:col + 128],
                              x1Tst[si][:, k * 128:(k + 1) * 128],
                              g_x1T[si].cw() if k == 0 else (),
                              g_x1d.pi(16), first=k == 0)
                    g_x1T[si].freed(g_x1d)
            bendw.clear()
            bendw.extend(list(g_x1p.cw()) + list(g_x1f.cw()) +
                         list(g_x1T[0].cw()) + list(g_x1T[1].cw()) +
                         list(g_x1d.cw()))

    def phase_c(self):
        c = self.ctx
        nc = self.nc
        TP, CP = self.TP, self.CP
        NTS = TP // 128
        NTT = TP // 512
        sb = lambda n, sh, ty: self.phstack.enter_context(nc.sbuf_tensor(n, sh, ty))
        x1T = sb("x1T", [128, 16 * TP], BF16)
        acc = sb("acc", [128, NTS * D], F32)
        gsl = sb("gsl", [128, 16 * 512], BF16)
        usl = sb("usl", [128, 16 * 512], BF16)
        dsl = sb("dsl", [128, 4 * D], BF16)
        fT = sb("fT", [128, 4 * TP], BF16)
        gT = sb("gT", [128, 512], BF16)
        uT = sb("uT", [128, 512], BF16)
        sdb_sb = sb("sdb_sb", [128, D], F32)
        xf = sb("xf0", [128, D], F32)
        pre2s = [sb(f"pre2{i}", [128, D], F32) for i in range(2)]

        g_x1T = Gate(c, "c.x1T")
        g_gs = Gate(c, "c.gs", kind="g")
        g_us = Gate(c, "c.us", kind="g")
        g_ds = Gate(c, "c.ds", kind="g")
        g_gT = Gate(c, "c.gT")
        g_uT = Gate(c, "c.uT")
        g_fT = Gate(c, "c.fT")
        g_acc = Gate(c, "c.acc")
        g_xf = Gate(c, "c.xf")
        g_outs = [Gate(c, f"c.out{i}") for i in range(2)]
        g_lvss = [Gate(c, f"c.lvs{i}") for i in range(2)]
        g_lsvs = [Gate(c, f"c.lsv{i}") for i in range(2)]
        g_pre2s = [Gate(c, f"c.pre2{i}") for i in range(2)]
        self.out_gate = Gate(c, "c.outd")
        g_sd = Gate(c, "c.sd")
        c.dma("sync", sdb_sb[:], self.sdb, (), g_sd.pi(16))

        NOB = INTER // 512
        finw = []

        def load_x1T(cp_):
            tb = cp_ * TP
            for hc in range(16):
                c.dma("sync", x1T[:, hc * TP:(hc + 1) * TP],
                      self.x1ts[hc * 128:(hc + 1) * 128, tb:tb + TP],
                      g_x1T.pw() if hc == 0 else (), g_x1T.pi(16),
                      first=hc == 0)
            return g_x1T.cw()

        x1r = load_x1T(0)
        for cp in range(CP):
            t0 = cp * TP
            for ob in range(NOB):
                for hc in range(16):
                    c.dma("gpsimd", gsl[:, hc * 512:(hc + 1) * 512],
                          self.wg8[hc * 128:(hc + 1) * 128,
                                   ob * 512:(ob + 1) * 512],
                          g_gs.pw() if hc == 0 else (), g_gs.pi(16),
                          first=hc == 0)
                    c.dma("gpsimd", usl[:, hc * 512:(hc + 1) * 512],
                          self.wu8[hc * 128:(hc + 1) * 128,
                                   ob * 512:(ob + 1) * 512],
                          g_us.pw() if hc == 0 else (), g_us.pi(16),
                          first=hc == 0)
                for ii in range(4):
                    c.dma("gpsimd", dsl[:, ii * D:(ii + 1) * D],
                          self.wd8[ob * 512 + ii * 128: ob * 512 + (ii + 1) * 128, :],
                          g_ds.pw() if ii == 0 else (), g_ds.pi(16),
                          first=ii == 0)
                for tt in range(NTT):
                    for ocl in range(4):
                        oc = ob * 4 + ocl
                        bk = self.pg[ocl]
                        for hc in range(16):
                            w = []
                            if hc == 0:
                                w = list(bk.pw()) + list(g_gs.cw())
                                if tt == 0 and ocl == 0 and ob == 0:
                                    w += list(x1r)
                            self.mm(self.pb[ocl][:, 0:512],
                                    gsl[:, hc * 512 + ocl * 128:
                                        hc * 512 + (ocl + 1) * 128],
                                    x1T[:, hc * TP + tt * 512:
                                        hc * TP + (tt + 1) * 512],
                                    hc == 0, hc == 15, w,
                                    bk.pi() if hc == 15 else ())
                            if hc == 15 and tt == NTT - 1 and ocl == 3:
                                g_gs.freed(bk)
                        c.emit("scalar", (lambda h, o=gT[:],
                                          i=self.pb[ocl][:, 0:512],
                                          sc=self.sgt_sb[:, oc:oc + 1]:
                                          h.activation(out=o, in_=i,
                                                       func=AF.Silu,
                                                       scale=sc)),
                               list(bk.cw()) + list(g_gT.pw()), g_gT.pi())
                        bk.freed(g_gT)
                        bu = self.pg[4 + ocl]
                        for hc in range(16):
                            w = []
                            if hc == 0:
                                w = list(bu.pw()) + list(g_us.cw())
                            self.mm(self.pb[4 + ocl][:, 0:512],
                                    usl[:, hc * 512 + ocl * 128:
                                        hc * 512 + (ocl + 1) * 128],
                                    x1T[:, hc * TP + tt * 512:
                                        hc * TP + (tt + 1) * 512],
                                    hc == 0, hc == 15, w,
                                    bu.pi() if hc == 15 else ())
                            if hc == 15 and tt == NTT - 1 and ocl == 3:
                                g_us.freed(bu)
                                if ob == NOB - 1:
                                    g_x1T.freed(bu)
                        c.emit("scalar", (lambda h, o=uT[:],
                                          i=self.pb[4 + ocl][:, 0:512],
                                          sc=self.sut_sb[:, oc:oc + 1]:
                                          h.activation(out=o, in_=i,
                                                       func=AF.Copy,
                                                       scale=sc)),
                               list(bu.cw()) + list(g_uT.pw()), g_uT.pi())
                        bu.freed(g_uT)
                        c.emit("vector", (lambda h,
                                          o=fT[:, ocl * TP + tt * 512:
                                               ocl * TP + (tt + 1) * 512],
                                          a=gT[:], b=uT[:]:
                                          h.tensor_mul(o, a, b)),
                               list(g_gT.cw()) + list(g_uT.cw()) +
                               (list(g_fT.pw()) if (tt == 0 and ocl == 0) else []),
                               g_fT.pi())
                        g_gT.freed(g_fT)
                        g_uT.freed(g_fT)
                ftr = g_fT.cw()
                for ts in range(NTS):
                    b0 = 4 * (ts % 2)
                    bks = self.pg[b0:b0 + 4]
                    for ii in range(4):
                        for si in range(4):
                            bk = bks[si]
                            w = []
                            if ii == 0:
                                w = list(bk.pw())
                                if si == 0:
                                    w += list(ftr) + list(g_ds.cw())
                            self.mm(self.pb[b0 + si][:, 0:512],
                                    fT[:, ii * TP + ts * 128:
                                       ii * TP + (ts + 1) * 128],
                                    dsl[:, ii * D + si * 512:
                                        ii * D + (si + 1) * 512],
                                    ii == 0, ii == 3, w,
                                    bk.pi() if ii == 3 else ())
                            if ii == 3 and si == 3 and ts == NTS - 1:
                                g_ds.freed(bk)
                                g_fT.freed(bk)
                    for si in range(4):
                        bk = bks[si]
                        dst = acc[:, ts * D + si * 512: ts * D + (si + 1) * 512]
                        if ob == 0:
                            c.emit("scalar",
                                   (lambda h, o=dst,
                                    i=self.pb[b0 + si][:, 0:512]:
                                    h.copy(o, i)),
                                   list(bk.cw()) +
                                   (list(g_acc.pw()) + list(finw)
                                    if (ts == 0 and si == 0) else []),
                                   g_acc.pi())
                        else:
                            c.emit("vector",
                                   (lambda h, o=dst,
                                    i=self.pb[b0 + si][:, 0:512]:
                                    h.tensor_add(o, i, o)),
                                   list(bk.cw()) +
                                   (list(g_acc.cw())
                                    if (ob == 1 and ts == 0 and si == 0)
                                    else []),
                                   g_acc.pi())
                        bk.freed(g_acc)
            if cp + 1 < CP:
                x1r = load_x1T(cp + 1)
            accr = g_acc.cw()
            for ts in range(NTS):
                i = ts % 2
                row = t0 + ts * 128
                c.dma("sync", xf[:], self.x1s[row:row + 128, :],
                      g_xf.pw(), g_xf.pi(16))
                c.emit("vector", (lambda h, o=pre2s[i][:],
                                  a=acc[:, ts * D:(ts + 1) * D], b=sdb_sb[:]:
                                  h.tensor_mul(o, a, b)),
                       list(accr) + list(g_sd.cw()) + list(g_pre2s[i].pw()),
                       g_pre2s[i].pi())
                c.emit("vector", (lambda h, o=pre2s[i][:], a=pre2s[i][:],
                                  b=xf[:]: h.tensor_add(o, a, b)),
                       list(g_xf.cw()) + list(g_pre2s[i].cw()),
                       g_pre2s[i].pi())
                g_xf.freed(g_pre2s[i])
                if ts == NTS - 1 and cp == CP - 1:
                    g_acc.freed(g_pre2s[i])
                self.ln_chain(pre2s[i][:], D, pre2s[i][:], g_lvss[i],
                              g_lsvs[i], g_outs[i], pre_gate=g_pre2s[i],
                              pre_ready=g_pre2s[i].cw(), stats_base=5 * i,
                              junk=acc[:, ts * D:(ts + 1) * D])
                c.dma("sync", self.out[row:row + 128, :], pre2s[i][:],
                      g_outs[i].cw(), self.out_gate.pi(16))
                g_outs[i].freed(self.out_gate)
                g_pre2s[i].freed(self.out_gate)
            finw = list(g_lsvs[0].cw()) + list(g_lsvs[1].cw())


_CACHE = {}


def _rope_tables(S):
    freqs = 1.0 / (10000.0 ** (np.arange(0, DH, 2, dtype=np.float64) / DH))
    ang = np.arange(S, dtype=np.float64)[:, None] * freqs[None, :]
    cos = np.concatenate([np.cos(ang), np.cos(ang)], -1).astype(np.float32)
    sin = np.concatenate([np.sin(ang), np.sin(ang)], -1).astype(np.float32)
    return cos, sin


def _q_rows(p, S):
    rows = []
    for m in range(S // 128):
        if m % 2 == p:
            rows.extend(range(128 * m, 128 * m + 128))
    return np.array(rows)


def _masks(p):
    out = np.zeros((128, 4096), dtype=NPBF)
    r = np.arange(128)[:, None]
    qq = np.arange(512)[None, :]
    for cc in range(8):
        vis = (128 * cc + r) <= (256 * (qq // 128) + 128 * p + (qq % 128))
        out[:, cc * 512:(cc + 1) * 512] = vis.astype(NPBF)
    return out


def kernel(L0_attn_x, L0_attn_W_dq, L0_attn_W_uq, L0_attn_W_qr, L0_attn_W_dkv,
           L0_attn_W_ukv, L0_attn_W_kr, L0_attn_W_o, L0_attn_q_ln_w,
           L0_attn_q_ln_b, L0_attn_kv_ln_w, L0_attn_kv_ln_b, L0_ffn_w_gate,
           L0_ffn_w_up, L0_ffn_w_down, L0_ffn_s_gate, L0_ffn_s_up,
           L0_ffn_s_down):
    S = 4096
    x = np.ascontiguousarray(np.asarray(L0_attn_x, dtype=np.float32))
    f32 = lambda a: np.ascontiguousarray(np.asarray(a, dtype=np.float32))
    wdq, wuq, wqr = f32(L0_attn_W_dq), f32(L0_attn_W_uq), f32(L0_attn_W_qr)
    wdkv, wukv, wkr = f32(L0_attn_W_dkv), f32(L0_attn_W_ukv), f32(L0_attn_W_kr)
    wo = f32(L0_attn_W_o)
    wqcat = np.ascontiguousarray(np.concatenate([wuq, wqr], axis=1))
    wg8 = np.ascontiguousarray(np.asarray(L0_ffn_w_gate, dtype=np.int8))
    wu8 = np.ascontiguousarray(np.asarray(L0_ffn_w_up, dtype=np.int8))
    wd8 = np.ascontiguousarray(np.asarray(L0_ffn_w_down, dtype=np.int8))
    sg = f32(L0_ffn_s_gate).reshape(64, 128).T.copy()
    su = f32(L0_ffn_s_up).reshape(64, 128).T.copy()
    sdb = np.ascontiguousarray(np.broadcast_to(f32(L0_ffn_s_down), (128, D)))
    qlw = np.ascontiguousarray(np.broadcast_to(f32(L0_attn_q_ln_w), (128, QP)))
    qlb = np.ascontiguousarray(np.broadcast_to(f32(L0_attn_q_ln_b), (128, QP)))
    kvlw = np.ascontiguousarray(np.broadcast_to(f32(L0_attn_kv_ln_w), (128, KV)))
    kvlb = np.ascontiguousarray(np.broadcast_to(f32(L0_attn_kv_ln_b), (128, KV)))
    cos, sin = _rope_tables(S)
    coskT = np.ascontiguousarray(cos.T)
    sinkT = np.ascontiguousarray(sin.T)
    identb = np.eye(128, dtype=NPBF)
    identf = np.eye(128, dtype=np.float32)

    if S not in _CACHE:
        _CACHE[S] = Builder(S).build()
    nc = _CACHE[S]

    in_maps = []
    rowmaps = []
    SH = S // 2
    for c in range(8):
        b, p = c >> 1, c & 1
        rows = _q_rows(p, S)
        rowmaps.append((b, rows))
        ks = slice(p * SH, (p + 1) * SH)
        in_maps.append({
            "xkv": np.ascontiguousarray(x[b][ks]),
            "xq": np.ascontiguousarray(x[b][rows]),
            "wdq": wdq, "wqcat": wqcat, "wdkv": wdkv, "wukv": wukv,
            "wkr": wkr, "wo": wo, "wg8": wg8, "wu8": wu8, "wd8": wd8,
            "sgt": sg, "sut": su, "sdb": sdb, "qlw": qlw, "qlb": qlb,
            "kvlw": kvlw, "kvlb": kvlb,
            "coskT": np.ascontiguousarray(coskT[:, ks]),
            "sinkT": np.ascontiguousarray(sinkT[:, ks]),
            "cosqT": np.ascontiguousarray(cos[rows].T),
            "sinqT": np.ascontiguousarray(sin[rows].T),
            "maskst": _masks(p), "identb": identb, "identf": identf,
        })
    res = run_bass_kernel_spmd(nc, in_maps, core_ids=list(range(8)))
    out = np.empty_like(x)
    for c in range(8):
        b, rows = rowmaps[c]
        out[b][rows] = res.results[c]["out"]
    return out



